# version 04
# baseline (speedup 1.0000x reference)
import numpy as np
import ml_dtypes
import concourse.bacc as bacc
import concourse.mybir as mybir
from concourse.bass_utils import run_bass_kernel_spmd
from contextlib import ExitStack

F32 = mybir.dt.float32
BF16 = mybir.dt.bfloat16
I16 = mybir.dt.int16
I32 = mybir.dt.int32
U32 = mybir.dt.uint32
U8 = mybir.dt.uint8
ALU = mybir.AluOpType
AF = mybir.ActivationFunctionType
AX = mybir.AxisListType

E = 1024
H = 16
WS = 512
NEXP = 8
FF = 4096
C = 640
CB = C // 16
KB = 1024


class Ctr:
    def __init__(self):
        self.v = 0

    def inc(self, n=1):
        self.v += n
        return self.v


def build_nc(NW=8, NITER=48, sim_single=False, debug=False):
    SEQ = NW * WS
    SB = SEQ // 16
    KGLOB = 8 * SEQ // NEXP
    nc = bacc.Bacc("TRN2", target_bir_lowering=False)

    dt = lambda n, s, d, k="ExternalInput", **kw: nc.dram_tensor(n, s, d, kind=k, **kw).ap()
    x_h = dt("x", [SEQ, E], F32)
    wT_h = {0: dt("wqT", [8, 128, E], F32), 1: dt("wkT", [8, 128, E], F32),
            2: dt("wvT", [8, 128, E], F32), 3: dt("woT", [8, 128, E], F32)}
    bq_h = dt("bq", [128, 8], F32)
    bk_h = dt("bk", [128, 8], F32)
    bv_row_h = dt("bv_row", [1, E], F32)
    bo_row_h = dt("bo_row", [1, E], F32)
    c_col_h = dt("c_col", [128, 8], F32)
    wmod_h = dt("wmod", [E, 6 * E], F32)
    Ws_h = dt("Ws", [E, NEXP], F32)
    bs_row_h = dt("bs_row", [1, NEXP], F32)
    W1_h = dt("W1", [NEXP, E, FF], F32)
    b1c_h = dt("b1c", [128, NEXP * 32], F32)
    W2_h = dt("W2", [NEXP, FF, E], F32)
    b2r_h = dt("b2r", [NEXP, E], F32)
    idbf_h = dt("id_bf", [128, 128], BF16)
    idf32_h = dt("id_f32", [128, 128], F32)
    tril_h = dt("tri_l", [128, 128], F32)
    triu_h = dt("tri_u", [128, 128], F32)
    sel_h = dt("selm", [128, 8], F32)
    selt_h = dt("selmT", [8, 128], F32)
    ones_bf_h = dt("ones_bf", [1, 128], BF16)

    out_h = dt("out", [SEQ, E], F32, k="ExternalOutput")

    h2m_hbm = dt("h2m_hbm", [SEQ, E], BF16, k="Internal")
    wfold_hbm = dt("wfold", [4, 8, 128, E], BF16, k="Internal")
    vT_hbm = dt("vT_hbm", [NEXP, SEQ], F32, k="Internal")
    rv_hbm = dt("rv_hbm", [NEXP, C], F32, k="Internal")
    mrows_hbm = dt("mrows_hbm", [4, E], BF16, k="Internal")
    cin_hbm = dt("cin", [8, SEQ], F32, k="Internal")
    if sim_single:
        cout_hbm = dt("cout", [64, SEQ], F32, k="Internal")
    else:
        cout_hbm = dt("cout", [64, SEQ], F32, k="Internal", addr_space="Shared")
    if debug:
        dbg_h1T = dt("dbg_h1T", [128, 8 * WS], BF16, k="Internal")
        dbg_q = dt("dbg_q", [128, 8 * WS], BF16, k="Internal")
        dbg_k = dt("dbg_k", [128, 8 * WS], BF16, k="Internal")
        dbg_v = dt("dbg_v", [128, 4 * 16 * 65], BF16, k="Internal")
        dbg_ao = dt("dbg_ao", [128, 4 * E], BF16, k="Internal")

    es = ExitStack()

    ARENA = 207 * KB
    arena_h = es.enter_context(nc.sbuf_tensor("arena", [128, ARENA], U8))
    arena_base = nc.lookup_mloc(arena_h).addr

    def alias(name, shape, dtype, off):
        h = nc.virtual_sbuf_tensor(name, shape, dtype)
        nc._tensor(h.name, h.shape, h.dtype, type="SB")
        ml = nc.lookup_mloc(h)
        ml.allocated = not nc.target_bir_lowering
        ml.addr = arena_base + off
        return h

    cur = [0]

    def nxt(nbytes, region=cur):
        o = region[0]
        region[0] = o + (nbytes + 63) // 64 * 64
        return o

    id_bf = alias("t_id_bf", [128, 128], BF16, nxt(256))
    id_f32 = alias("t_id_f32", [128, 128], F32, nxt(512))
    tri_l = alias("t_tri_l", [128, 128], F32, nxt(512))
    tri_u = alias("t_tri_u", [128, 128], F32, nxt(512))
    selm = alias("t_selm", [128, 8], F32, nxt(32))
    selmT = alias("t_selmT", [8, 128], F32, nxt(512))
    ones_bf = alias("t_ones_bf", [1, 128], BF16, nxt(256))
    c_col = alias("t_c_col", [128, 8], F32, nxt(32))
    silc = alias("t_silc", [128, 8], F32, nxt(32))
    modT = alias("t_modT", [128, 48], F32, nxt(192))
    ca1 = alias("t_ca1", [128, 8], F32, nxt(32))
    gaC = alias("t_gaC", [128, 8], F32, nxt(32))
    gbC = alias("t_gbC", [128, 8], F32, nxt(32))
    absga = alias("t_absga", [128, 8], F32, nxt(32))
    absgb = alias("t_absgb", [128, 8], F32, nxt(32))
    bq_col = alias("t_bq_col", [128, 8], F32, nxt(32))
    bk_col = alias("t_bk_col", [128, 8], F32, nxt(32))
    czero = alias("t_czero", [128, 1], F32, nxt(4))
    cone = alias("t_cone", [128, 1], F32, nxt(4))
    chalf = alias("t_chalf", [128, 1], F32, nxt(4))
    cquart = alias("t_cquart", [128, 1], F32, nxt(4))
    ctwo = alias("t_ctwo", [128, 1], F32, nxt(4))
    crecE = alias("t_crecE", [128, 1], F32, nxt(4))
    ceps = alias("t_ceps", [128, 1], F32, nxt(4))
    ck = alias("t_ck", [128, 1], F32, nxt(4))
    cone8 = alias("t_cone8", [8, 1], F32, nxt(4))
    chalf8 = alias("t_chalf8", [8, 1], F32, nxt(4))
    lo_t = alias("t_lo", [128, 1], F32, nxt(4))
    hi_t = alias("t_hi", [128, 1], F32, nxt(4))
    tm_t = alias("t_tm", [128, 1], F32, nxt(4))
    ge_t = alias("t_ge", [128, 1], F32, nxt(4))
    e1_t = alias("t_e1", [128, 1], F32, nxt(4))
    e2_t = alias("t_e2", [128, 1], F32, nxt(4))
    cnt_t = alias("t_cnt", [128, 1], F32, nxt(4))
    cntE = alias("t_cntE", [8, 1], F32, nxt(4))
    thr8 = alias("t_thr8", [8, 1], F32, nxt(4))
    mrow_bf = alias("t_mrow_bf", [8, 128], BF16, nxt(256))
    rows_sb = [alias("t_rows%d" % i, [1, E], BF16, nxt(E * 2)) for i in range(4)]
    bs_bf = alias("t_bs_bf", [1, NEXP], BF16, nxt(64))
    bv_sb = alias("t_bvsb", [1, E], BF16, nxt(E * 2))
    bo_sb = alias("t_bosb", [1, E], BF16, nxt(E * 2))
    Wsb = alias("t_Wsb", [128, 64], BF16, nxt(128))
    b1c = alias("t_b1c", [128, NEXP * 32], F32, nxt(NEXP * 32 * 4))
    b2row_bf = alias("t_b2row", [1, E], BF16, nxt(2 * KB))
    mean_t = alias("t_mean", [128, 1], F32, nxt(4))
    var_t = alias("t_var", [128, 1], F32, nxt(4))
    rstd_t = alias("t_rstd", [128, 1], F32, nxt(4))
    nmr_t = alias("t_nmr", [128, 1], F32, nxt(4))
    sq_t = alias("t_sq", [128, E], BF16, nxt(2 * KB))
    c16th = alias("t_c16th", [8, 1], F32, nxt(4))
    rz_t = alias("t_rz", [128, 1], F32, nxt(4))
    kpad = alias("t_kpad", [128, 128], BF16, nxt(256))
    vpad = alias("t_vpad", [128, 16 * 65], BF16, nxt(16 * 65 * 2))
    gaTb = alias("t_gaTb", [128, E], BF16, nxt(2 * KB))
    gbTb = alias("t_gbTb", [128, E], BF16, nxt(2 * KB))
    cbTb = alias("t_cbTb", [128, E], BF16, nxt(2 * KB))
    sbTb = alias("t_sbTb", [128, E], BF16, nxt(2 * KB))
    bvTb = alias("t_bvTb", [128, E], BF16, nxt(2 * KB))
    boTb = alias("t_boTb", [128, E], BF16, nxt(2 * KB))
    
    idx_all = alias("t_idx_all", [128, NEXP * CB], I16, nxt(NEXP * CB * 2))
    rvsm_all = alias("t_rvsm", [128, NEXP * (C // 128)], F32, nxt(NEXP * (C // 128) * 4))
    probe_tok = alias("t_probe_tok", [128, 8], F32, nxt(32))
    v16 = alias("t_v16", [16, SB], F32, nxt(SB * 4))
    cmp16 = alias("t_cmp16", [16, SB], F32, nxt(SB * 4))
    clean16 = alias("t_clean16", [16, SB], F32, nxt(SB * 4))
    m16 = alias("t_m16", [16, SB], I32, nxt(SB * 4))
    zz16 = alias("t_zz16", [16, SB], F32, nxt(SB * 4))
    ids16 = alias("t_ids16", [16, SB], F32, nxt(SB * 4))
    idxlo = alias("t_idxlo", [16, CB], I16, nxt(CB * 2))
    idflo = alias("t_idflo", [16, CB], F32, nxt(CB * 4))
    rv16 = alias("t_rv16", [16, CB], F32, nxt(CB * 4))
    nf_t = alias("t_nf", [1, 1], U32, nxt(4))
    nff_t = alias("t_nff", [1, 1], F32, nxt(4))
    nfb_t = alias("t_nfb", [16, 1], F32, nxt(4))
    b2Tsb = alias("t_b2Tsb", [128, E], BF16, nxt(2 * KB))
    pstage = alias("t_pstage", [8, 128], F32, nxt(512))
    wsstage = alias("t_wsstage", [128, 64], F32, nxt(256))

    shared_end = cur[0]
    REG0 = (shared_end + 63) // 64 * 64
    assert REG0 <= 49 * KB, REG0

    acur = [REG0]
    xw = alias("t_xw", [128, 2 * 4 * E], F32, nxt(32 * KB, acur))
    wslot = alias("t_wslot", [128, 2 * 8 * E], BF16, nxt(32 * KB, acur))
    kroll_off = nxt(24 * KB, acur)
    kroll = alias("t_kroll", [128, 3 * 8 * WS], BF16, kroll_off)
    foldtile = alias("t_foldtile", [128, E], BF16, kroll_off)
    vroll_off = nxt(3 * 4 * 16 * 65 * 2, acur)
    vroll = alias("t_vroll", [128, 3 * 4 * 16 * 65], BF16, vroll_off)
    wstage = alias("t_wstage", [128, E], F32, vroll_off)
    rowstage = alias("t_rowstage", [1, E], F32, vroll_off + 4 * KB)
    qbuf = alias("t_qbuf", [128, 8 * WS], BF16, nxt(8 * KB, acur))
    h1T = alias("t_h1T", [128, 8 * WS], BF16, nxt(8 * KB, acur))
    attnp = alias("t_attnp", [128, 2 * WS], BF16, nxt(2 * KB, acur))
    ao_off = nxt(8 * KB, acur)
    attnout = alias("t_attnout", [128, 4 * E], BF16, ao_off)
    h1t = alias("t_h1t", [128, 2 * E], BF16, ao_off)
    aoT = alias("t_aoT", [128, 8 * 128], BF16, nxt(2 * KB, acur))
    h2mt = alias("t_h2mt", [128, 2 * E], BF16, nxt(4 * KB, acur))
    assert acur[0] <= ARENA, acur[0]

    mcur = [REG0]
    outacc = alias("t_outacc", [128, 5 * E], F32, nxt(20 * KB, mcur))
    xin_off = nxt(2 * 8 * C * 2, mcur)
    xinT = alias("t_xinT", [128, 2 * 8 * C], BF16, xin_off)
    bisdump = alias("t_bisdump", [128, SEQ // 2], BF16, xin_off)
    search = alias("t_search", [128, SEQ // 2], F32, nxt(SEQ // 2 * 4, mcur))
    W1_off = nxt(32 * KB, mcur)
    W1blk = alias("t_W1blk", [128, 2 * 8 * KB], BF16, W1_off)
    probs_sb = alias("t_probs_sb", [8, SEQ], F32, W1_off)
    W2blk = alias("t_W2blk", [128, 2 * 8 * KB], BF16, nxt(32 * KB, mcur))
    vTt = alias("t_vTt", [8, SEQ], F32, nxt(SEQ * 4, mcur))
    selT_off = nxt(max(SEQ * 4, 2 * 8 * C * 2), mcur)
    selT = alias("t_selT", [8, SEQ], F32, selT_off)
    hexp = alias("t_hexp", [128, 2 * 8 * C], BF16, selT_off)
    assert mcur[0] <= ARENA, mcur[0]

    pb = [es.enter_context(nc.psum_tensor("pb%d" % i, [128, 512], F32)) for i in range(7)]
    pbf = es.enter_context(nc.psum_tensor("pbf", [128, 1024], BF16))

    sem = lambda n: es.enter_context(nc.semaphore(n))
    sT = sem("sT"); cT = Ctr()
    sV = sem("sV"); cV = Ctr()
    sS = sem("sS"); cS = Ctr()
    sG = sem("sG"); cG = Ctr()
    qS = sem("qS"); cQS = Ctr()
    qA = sem("qA"); cQA = Ctr()
    csem = sem("csem")
    gsems = [sem("gs%d" % e) for e in range(NEXP)]
    ssems = [sem("ss%d" % e) for e in range(NEXP)]
    gq0 = sem("gq0"); cGQ0 = Ctr()
    qX = sem("qX"); cQX = Ctr()
    qW = [sem("qW%d" % i) for i in range(4)]
    cQW = [Ctr() for _ in range(4)]
    qC = sem("qC"); cQC = Ctr()
    qF = sem("qF"); cQF = Ctr()
    qO = sem("qO"); cQO = Ctr()
    qH = sem("qH"); cQH = Ctr()
    qP = sem("qP"); cQP = Ctr()
    qVT = sem("qVT"); cQVT = Ctr()
    gq2 = sem("gq2"); cGQ2 = Ctr()
    gq3 = sem("gq3"); cGQ3 = Ctr()

    block = es.enter_context(nc.Block())
    T, Vv, Sc, Sy, Gp = block.tensor, block.vector, block.scalar, block.sync, block.gpsimd

    bank_v = {i: 0 for i in range(8)}
    bank_s = {i: 0 for i in range(8)}

    def bank_wait(tensor, i):
        if bank_v[i]:
            tensor.wait_ge(sV, bank_v[i])
        if bank_s[i]:
            tensor.wait_ge(sS, bank_s[i])

    def pe_bump(tensor):
        tensor.matmul(pb[6][0:1, 0:1], id_f32[0:1, 0:1], id_f32[0:1, 0:1],
                      start=True, stop=True).then_inc(sT, 1)
        cT.inc()

    def p0_loads(sync):
        for h_, t_ in [(idbf_h, id_bf), (idf32_h, id_f32), (tril_h, tri_l),
                       (triu_h, tri_u), (sel_h, selm), (selt_h, selmT),
                       (ones_bf_h, ones_bf), (c_col_h, c_col),
                       (bq_h, bq_col), (bk_h, bk_col), (b1c_h, b1c)]:
            sync.dma_start(out=t_[:, :], in_=h_[:, :]).then_inc(qS, 16)
            cQS.inc(16)
    Sy(p0_loads)
    p0_loads_done = cQS.v

    def p0_gload1(sync):
        sync.dma_start(out=wsstage[:, :].rearrange("p (c e) -> p c e", c=8),
                       in_=Ws_h[:, :].rearrange("(c p) e -> p c e", p=128)).then_inc(qF, 16)
        cQF.inc(16)
    Sy(p0_gload1)
    ws_ld = cQF.v

    def p0_gcast1(vector):
        vector.wait_ge(qF, ws_ld)
        vector.tensor_copy(Wsb[:, :], wsstage[:, :]).then_inc(sV, 1)
        cV.inc()
    Vv(p0_gcast1)
    wsb_done = cV.v

    rows_cast = []
    for (rh, rt, n) in [(bs_row_h, bs_bf, NEXP), (bv_row_h, bv_sb, E), (bo_row_h, bo_sb, E)]:
        def rld(sync, rh=rh, n=n, gate=(rows_cast[-1] if rows_cast else wsb_done)):
            sync.wait_ge(sV, gate)
            sync.dma_start(out=rowstage[:, 0:n], in_=rh[:, :]).then_inc(qF, 16)
            cQF.inc(16)
        Sy(rld)
        rlv = cQF.v

        def rcast(vector, rt=rt, n=n, rlv=rlv):
            vector.wait_ge(qF, rlv)
            vector.tensor_copy(rt[:, 0:n], rowstage[:, 0:n]).then_inc(sV, 1)
            cV.inc()
        Vv(rcast)
        rows_cast.append(cV.v)
    p0_gloads_done = rows_cast[-1]

    xload_marks = {}

    def p0_xload(scalar):
        scalar.dma_start(out=xw[:, 0:4 * E].rearrange("p (t f) -> p t f", t=4),
                         in_=x_h[0:WS, :].rearrange("(t p) f -> p t f", p=128)).then_inc(qX, 16)
        cQX.inc(16)
    Sc(p0_xload)
    xload_marks[0] = cQX.v
    rows_done = p0_gloads_done

    def p0_consts(vector):
        vector.memset(czero[:, :], 0.0)
        vector.memset(cone[:, :], 1.0)
        vector.memset(chalf[:, :], 0.5)
        vector.memset(cquart[:, :], 0.25)
        vector.memset(ctwo[:, :], 2.0)
        vector.memset(crecE[:, :], 1.0 / E)
        vector.memset(ceps[:, :], 1e-5)
        vector.memset(ck[:, :], float(KGLOB))
        vector.memset(cone8[:, :], 1.0)
        vector.memset(chalf8[:, :], 0.5)
        vector.memset(c16th[:, :], 1.0 / 16.0)
        vector.memset(lo_t[:, :], 0.0)
        vector.memset(hi_t[:, :], 1.0)
        vector.memset(tm_t[:, :], 0.5)
        vector.memset(kpad[:, :], -1.0)
        vector.memset(vpad[:, :], -1.0)
        vector.memset(zz16[:, :], 0.0)
        vector.memset(idx_all[:, :], 0)
        vector.memset(nfb_t[:, :], 0.0)
        vector.memset(nff_t[:, :], 0.0)
        vector.memset(cmp16[:, :], -1.0)
        vector.drain()
        vector.memset(vpad[:, :].rearrange("p (h x) -> p h x", x=65)[:, :, 64:65], 1.0)
        vector.drain()
        vector.tensor_copy(rz_t[:, :], czero[:, :]).then_inc(sV, 1)
        cV.inc()
    Vv(p0_consts)
    consts_done = cV.v

    def p0_silu(scalar):
        scalar.wait_ge(qS, p0_loads_done)
        scalar.activation(out=silc[:, :], in_=c_col[:, :], func=AF.Silu).then_inc(sS, 1)
        cS.inc()
    Sc(p0_silu)
    silu_done = cS.v

    modmm_last = [0]
    for j in range(48):
        def ld(sync, j=j):
            if modmm_last[0]:
                sync.wait_ge(sT, modmm_last[0])
            if j == 0:
                sync.wait_ge(qS, p0_loads_done)
            sync.dma_start(out=wstage[:, :].rearrange("p (d f) -> p d f", d=8),
                           in_=wmod_h[:, j * 128:(j + 1) * 128].rearrange("(d p) f -> p d f", p=128)
                           ).then_inc(qS, 16)
            cQS.inc(16)
        Sy(ld)
        ldv = cQS.v

        def mm(tensor, j=j, ldv=ldv):
            tensor.wait_ge(qS, ldv)
            if j == 0:
                tensor.wait_ge(sS, silu_done)
            bank_wait(tensor, 0)
            for d in range(8):
                tensor.matmul(pb[0][:, (j % 4) * 8 + d:(j % 4) * 8 + d + 1],
                              wstage[:, d * 128:(d + 1) * 128], silc[:, d:d + 1],
                              start=True, stop=True)
            pe_bump(tensor)
        T(mm)
        modmm_last[0] = cT.v

        def rd(vector, j=j, mmv=cT.v):
            vector.wait_ge(sT, mmv)
            vector.tensor_reduce(modT[:, j:j + 1],
                                 pb[0][:, (j % 4) * 8:(j % 4) * 8 + 8],
                                 axis=AX.X, op=ALU.add).then_inc(sV, 1)
            cV.inc()
        Vv(rd)
        if j % 4 == 3:
            bank_v[0] = cV.v
    mod_mm_done = modmm_last[0]
    mod_done = cV.v

    def p0_clamp1(vector):
        vector.wait_ge(sV, mod_done)
        vector.tensor_scalar(ca1[:, :], modT[:, 8:16], cone[:, :], None, ALU.add)
        vector.tensor_scalar(gaC[:, :], modT[:, 16:24], czero[:, :], None, ALU.is_ge)
        vector.tensor_scalar(gbC[:, :], modT[:, 40:48], czero[:, :], None, ALU.is_ge)
        vector.drain()
        vector.tensor_scalar(gaC[:, :], gaC[:, :], ctwo[:, :], cone[:, :], ALU.mult, ALU.subtract)
        vector.tensor_scalar(gbC[:, :], gbC[:, :], ctwo[:, :], cone[:, :], ALU.mult, ALU.subtract)
        vector.drain()
        vector.tensor_copy(e1_t[:, :], czero[:, :]).then_inc(sV, 1)
        cV.inc()
    Vv(p0_clamp1)
    clamp1_done = cV.v

    def p0_abs(scalar):
        scalar.wait_ge(sV, clamp1_done)
        scalar.activation(out=absga[:, :], in_=modT[:, 16:24], func=AF.Abs)
        scalar.activation(out=absgb[:, :], in_=modT[:, 40:48], func=AF.Abs).then_inc(sS, 1)
        cS.inc()
    Sc(p0_abs)
    abs_done = cS.v

    def p0_clamp2(vector):
        vector.wait_ge(sS, abs_done)
        vector.tensor_scalar(absga[:, :], absga[:, :], cquart[:, :], None, ALU.max)
        vector.tensor_scalar(absgb[:, :], absgb[:, :], cquart[:, :], None, ALU.max)
        vector.drain()
        vector.tensor_tensor(gaC[:, :], gaC[:, :], absga[:, :], ALU.mult)
        vector.tensor_tensor(gbC[:, :], gbC[:, :], absgb[:, :], ALU.mult)
        vector.drain()
        vector.tensor_copy(e1_t[:, :], czero[:, :]).then_inc(sV, 1)
        cV.inc()
    Vv(p0_clamp2)
    clamp_done = cV.v

    bc_jobs = [(0, gaC, None, gaTb), (1, gbC, None, gbTb),
               (2, modT, slice(32, 40), cbTb), (3, modT, slice(24, 32), sbTb)]
    mrow_fr = [0]
    for (ri, src, sl, dstb) in bc_jobs:
        def tp(tensor, src=src, sl=sl):
            tensor.wait_ge(sV, clamp_done)
            bank_wait(tensor, 2)
            s_ap = src[:, sl] if sl is not None else src[:, :]
            tensor.transpose(pb[2][0:8, 0:128], s_ap, id_f32[:, :])
            pe_bump(tensor)
        T(tp)
        tpv = cT.v

        def ev(vector, tpv=tpv, prev_fr=mrow_fr[0]):
            vector.wait_ge(sT, tpv)
            if prev_fr:
                vector.wait_ge(qS, prev_fr)
            vector.tensor_copy(mrow_bf[:, :], pb[2][0:8, 0:128]).then_inc(sV, 1)
            cV.inc()
        Vv(ev)
        bank_v[2] = cV.v
        evv = cV.v

        def fold_row(sync, ri=ri, evv=evv):
            sync.wait_ge(sV, evv)
            sync.dma_start(out=mrows_hbm[ri:ri + 1, :].rearrange("r (d f) -> (r d) f", d=8),
                           in_=mrow_bf[:, :]).then_inc(qS, 16)
            cQS.inc(16)
            sync.wait_ge(qS, cQS.v)
            sync.dma_start(out=rows_sb[ri][:, :], in_=mrows_hbm[ri:ri + 1, :]).then_inc(qS, 16)
            cQS.inc(16)
        Sy(fold_row)
        frv = cQS.v
        mrow_fr[0] = frv

        def bc(gpsimd, ri=ri, frv=frv, dstb=dstb):
            gpsimd.wait_ge(qS, frv)
            gpsimd.partition_broadcast(dstb[:, :], rows_sb[ri][:, :], channels=128)
            gpsimd.drain()
            gpsimd.memset(nfb_t[0:1, 0:1], 0.0).then_inc(sG, 1)
            cG.inc()
        Gp(bc)
    for (rt, dstb) in [(bv_sb, bvTb), (bo_sb, boTb)]:
        def bc2(gpsimd, rt=rt, dstb=dstb):
            gpsimd.wait_ge(sV, rows_done)
            gpsimd.partition_broadcast(dstb[:, :], rt[:, :], channels=128)
            gpsimd.drain()
            gpsimd.memset(nfb_t[0:1, 0:1], 0.0).then_inc(sG, 1)
            cG.inc()
        Gp(bc2)
    bcast_g_done = cG.v

    def bcfin(vector):
        vector.wait_ge(sG, bcast_g_done)
        vector.tensor_copy(rz_t[:, :], czero[:, :]).then_inc(sV, 1)
        cV.inc()
    Vv(bcfin)
    bcast_done = cV.v

    fold_free_v = [0]
    fold_free_t = [0]
    for (mi, do_fold, bcol) in [(0, True, bq_col), (1, True, bk_col),
                                (2, True, None), (3, False, None)]:
        mv_last = 0
        for d in range(8):
            def ld(scalar, mi=mi, d=d):
                if fold_free_v[0]:
                    scalar.wait_ge(sV, fold_free_v[0])
                if fold_free_t[0]:
                    scalar.wait_ge(sT, fold_free_t[0])
                if mi == 0 and d == 0:
                    scalar.wait_ge(sT, mod_mm_done)
                scalar.dma_start(out=wstage[:, :], in_=wT_h[mi][d, :, :]).then_inc(qA, 16)
                cQA.inc(16)
            Sc(ld)
            ldv = cQA.v

            if bcol is not None:
                def mv(tensor, d=d, ldv=ldv):
                    tensor.wait_ge(qA, ldv)
                    tensor.wait_ge(sV, clamp_done)
                    bank_wait(tensor, 1)
                    for fc in range(8):
                        tensor.matmul(pb[1][:, fc * 8 + d:fc * 8 + d + 1],
                                      wstage[:, fc * 128:(fc + 1) * 128],
                                      modT[:, d:d + 1], start=True, stop=True)
                    pe_bump(tensor)
                T(mv)
                mv_last = cT.v
                fold_free_t[0] = cT.v

            def fold(vector, mi=mi, d=d, ldv=ldv, do_fold=do_fold):
                vector.wait_ge(qA, ldv)
                if do_fold:
                    vector.wait_ge(sV, clamp_done)
                    vector.tensor_scalar(foldtile[:, :], wstage[:, :], ca1[:, d:d + 1],
                                         None, ALU.mult).then_inc(sV, 1)
                else:
                    vector.tensor_copy(foldtile[:, :], wstage[:, :]).then_inc(sV, 1)
                cV.inc()
            Vv(fold)
            fv = cV.v

            def st(scalar, mi=mi, d=d, fv=fv):
                scalar.wait_ge(sV, fv)
                scalar.dma_start(out=wfold_hbm[mi, d, :, :], in_=foldtile[:, :]).then_inc(qA, 16)
                cQA.inc(16)
                fold_free_v[0] = fv
            Sc(st)
        if bcol is not None:
            def bias(vector, bcol=bcol, mv_last=mv_last):
                vector.wait_ge(sT, mv_last)
                vector.wait_ge(qS, p0_loads_done)
                vector.tensor_reduce(probe_tok[:, 0:8],
                                     pb[1][:, 0:64].rearrange("p (fc d) -> p fc d", d=8),
                                     axis=AX.X, op=ALU.add)
                vector.drain()
                vector.tensor_tensor(bcol[:, :], bcol[:, :], probe_tok[:, 0:8], ALU.add)
                vector.drain()
                vector.tensor_copy(rz_t[:, :], czero[:, :]).then_inc(sV, 1)
                cV.inc()
            Vv(bias)
            bank_v[1] = cV.v
    fold_done_qA = cQA.v

    def p0_vones(vector):
        vector.wait_ge(qA, fold_done_qA)
        vector.memset(vroll[:, :].rearrange("p (c x) -> p c x", x=65)[:, :, 64:65], 1.0)
        vector.drain()
        vector.tensor_copy(rz_t[:, :], czero[:, :]).then_inc(sV, 1)
        cV.inc()
    Vv(p0_vones)
    vones_done = cV.v

    WQ, WK, WV, WO = 0, 1, 2, 3
    wslot_ap = lambda s: wslot[:, s * 8 * E:(s + 1) * 8 * E]
    wload_marks = {}
    wslot_last = {}

    def load_w(sync, w, m, s, wait_t):
        if wait_t:
            sync.wait_ge(sT, wait_t)
        prior = wslot_last.get(s)
        if prior is not None:
            sync.wait_ge(qW[prior[0]], prior[1])
        sync.wait_ge(qA, fold_done_qA)
        sync.dma_start(out=wslot_ap(s).rearrange("p (d f) -> p d f", d=8),
                       in_=wfold_hbm[m, :, :, :].rearrange("d p f -> p d f")).then_inc(qW[m], 16)
        cQW[m].inc(16)
        wload_marks[(w, m)] = cQW[m].v
        wslot_last[s] = (m, cQW[m].v)

    def p1_init_loads(sync):
        load_w(sync, 0, WK, 0, None)
        load_w(sync, 0, WV, 1, None)
    Sy(p1_init_loads)

    h1_ev = {}
    qk_ev = {}
    v_ev = {}
    q_ev = {}
    x1_written = {}
    h2m_written = {}
    xslot_guard = {}
    attn_last_av = {0: 0, 1: 0}
    probsT_parts = []

    def emit_h1(w):
        for tt in range(4):
            xt = xw[:, ((w % 2) * 4 + tt) * E:((w % 2) * 4 + tt + 1) * E]

            def stats(vector, xt=xt, w=w, tt=tt):
                vector.wait_ge(qX, xload_marks[w])
                if w == 0 and tt == 0:
                    vector.wait_ge(sV, consts_done)
                vector.tensor_reduce(mean_t[:, :], xt, axis=AX.X, op=ALU.add, negate=True)
                vector.tensor_tensor(sq_t[:, :], xt, xt, ALU.mult)
                vector.drain()
                vector.tensor_reduce(var_t[:, :], sq_t[:, :], axis=AX.X, op=ALU.add)
                vector.drain()
                vector.tensor_scalar(mean_t[:, :], mean_t[:, :], crecE[:, :], None, ALU.mult)
                vector.tensor_scalar(var_t[:, :], var_t[:, :], crecE[:, :], None, ALU.mult)
                vector.drain()
                vector.tensor_tensor(sq_t[:, 0:1], mean_t[:, :], mean_t[:, :], ALU.mult)
                vector.drain()
                vector.tensor_tensor(var_t[:, :], var_t[:, :], sq_t[:, 0:1], ALU.subtract).then_inc(sV, 1)
                cV.inc()
            Vv(stats)
            stv = cV.v

            def rstd(scalar, stv=stv):
                scalar.wait_ge(sV, stv)
                scalar.activation(out=rstd_t[:, :], in_=var_t[:, :], func=AF.Sqrt,
                                  bias=ceps[:, :]).then_inc(sS, 1)
                cS.inc()
            Sc(rstd)
            rsv = cS.v

            def norm(vector, xt=xt, tt=tt, rsv=rsv):
                vector.wait_ge(sS, rsv)
                g = h1t_guard.get(tt % 2)
                if g:
                    vector.wait_ge(sT, g)
                gg = h1tg_g.get(tt % 2)
                if gg:
                    vector.wait_ge(sG, gg)
                vector.reciprocal(rstd_t[:, :], rstd_t[:, :])
                vector.drain()
                vector.tensor_scalar(h1t[:, (tt % 2) * E:(tt % 2 + 1) * E], xt,
                                     mean_t[:, :], rstd_t[:, :],
                                     ALU.add, ALU.mult).then_inc(sV, 1)
                cV.inc()
            Vv(norm)
            nv = cV.v

            def tp(tensor, tt=tt, nv=nv):
                tensor.wait_ge(sV, nv)
                bank_wait(tensor, 7)
                for fc in range(8):
                    tensor.transpose(pbf[:, fc * 128:(fc + 1) * 128],
                                     h1t[:, (tt % 2) * E + fc * 128:(tt % 2) * E + (fc + 1) * 128],
                                     id_bf[:, :])
                pe_bump(tensor)
            T(tp)
            h1t_guard[tt % 2] = cT.v
            tpv = cT.v

            def tev(vector, tt=tt, tpv=tpv, w=w):
                vector.wait_ge(sT, tpv)
                if debug and w == 1 and tt == 0:
                    vector.wait_ge(qP, 64)
                vector.tensor_copy(
                    h1T[:, :].rearrange("p (d t) -> p d t", d=8)[:, :, tt * 128:(tt + 1) * 128],
                    pbf[:, :].rearrange("p (d t) -> p d t", d=8)).then_inc(sV, 1)
                cV.inc()
            Vv(tev)
            bank_v[7] = cV.v
        h1_ev[w] = cV.v

    h1t_guard = {}
    h1tg_g = {}

    def emit_proj_K(w):
        for fc in range(8):
            bank = fc % 2

            def mm(tensor, fc=fc, w=w, bank=bank):
                if fc == 0:
                    tensor.wait_ge(qW[WK], wload_marks[(w, WK)])
                    tensor.wait_ge(sV, h1_ev[w])
                bank_wait(tensor, bank)
                for d in range(8):
                    tensor.matmul(pb[bank][:, :],
                                  wslot_ap(0)[:, d * E + fc * 128: d * E + (fc + 1) * 128],
                                  h1T[:, d * WS:(d + 1) * WS],
                                  start=(d == 0), stop=(d == 7))
                pe_bump(tensor)
            T(mm)
            mmv = cT.v

            def ev(vector, fc=fc, w=w, mmv=mmv, bank=bank):
                vector.wait_ge(sT, mmv)
                vector.tensor_scalar(
                    kroll[:, ((w % 3) * 8 + fc) * WS:((w % 3) * 8 + fc + 1) * WS],
                    pb[bank][:, :], bk_col[:, fc:fc + 1], None, ALU.add).then_inc(sV, 1)
                cV.inc()
            Vv(ev)
            bank_v[bank] = cV.v
        qk_ev[w] = cV.v

    def emit_proj_V(w):
        for tt in range(4):
            for half in range(2):
                bank = (tt * 2 + half) % 2

                def mm(tensor, tt=tt, half=half, w=w, bank=bank):
                    if tt == 0 and half == 0:
                        tensor.wait_ge(qW[WV], wload_marks[(w, WV)])
                    bank_wait(tensor, bank)
                    for d in range(8):
                        tensor.matmul(pb[bank][:, :],
                                      h1T[:, d * WS + tt * 128: d * WS + (tt + 1) * 128],
                                      wslot_ap(1)[:, d * E + half * 512: d * E + (half + 1) * 512],
                                      start=(d == 0), stop=(d == 7))
                    pe_bump(tensor)
                T(mm)
                mmv = cT.v

                def ev(vector, tt=tt, half=half, w=w, mmv=mmv, bank=bank):
                    vector.wait_ge(sT, mmv)
                    if tt == 0 and half == 0:
                        vector.wait_ge(sV, max(bcast_done, vones_done))
                    dst = vroll[:, ((w % 3) * 4 + tt) * 16 * 65:((w % 3) * 4 + tt + 1) * 16 * 65]
                    dst = dst.rearrange("p (h x) -> p h x", x=65)[:, half * 8:(half + 1) * 8, 0:64]
                    vector.tensor_tensor(
                        dst, pb[bank][:, :].rearrange("p (h f) -> p h f", f=64),
                        bvTb[:, half * 512:(half + 1) * 512].rearrange("p (h f) -> p h f", f=64),
                        ALU.add).then_inc(sV, 1)
                    cV.inc()
                Vv(ev)
                bank_v[bank] = cV.v
        v_ev[w] = cV.v

    def emit_proj_Q(w):
        for fc in range(8):
            bank = fc % 2

            def mm(tensor, fc=fc, w=w, bank=bank):
                if fc == 0:
                    tensor.wait_ge(qW[WQ], wload_marks[(w, WQ)])
                    if debug and w == 1:
                        tensor.wait_ge(qP, 64)
                bank_wait(tensor, bank)
                for d in range(8):
                    tensor.matmul(pb[bank][:, :],
                                  wslot_ap(0)[:, d * E + fc * 128: d * E + (fc + 1) * 128],
                                  h1T[:, d * WS:(d + 1) * WS],
                                  start=(d == 0), stop=(d == 7))
                pe_bump(tensor)
            T(mm)
            mmv = cT.v

            def ev(vector, fc=fc, mmv=mmv, bank=bank):
                vector.wait_ge(sT, mmv)
                vector.tensor_scalar(qbuf[:, fc * WS:(fc + 1) * WS],
                                     pb[bank][:, :], bq_col[:, fc:fc + 1],
                                     None, ALU.add).then_inc(sV, 1)
                cV.inc()
            Vv(ev)
            bank_v[bank] = cV.v
        q_ev[w] = cV.v

    def emit_attend(wa):
        lastc = {}
        for d_ in range(4):
            lc = 0
            for c_ in range(12):
                win_ = wa - 1 + c_ // 4
                pad_ = (win_ < 0) or (win_ >= NW)
                if pad_ or (0 <= c_ - d_ <= 8):
                    lc = c_
            lastc[d_] = lc
        for h in range(H):
            fcH, hh = h // 2, h % 2
            av_started = [False] * 4

            def headwaits(tensor, wa=wa, h=h):
                if h == 0:
                    kw = wa + 1 if wa + 1 < NW else wa
                    tensor.wait_ge(sV, qk_ev[kw])
                    tensor.wait_ge(sV, v_ev[kw])
                    tensor.wait_ge(sV, q_ev[wa])
            T(headwaits)

            for c in range(12):
                win = wa - 1 + c // 4
                pad = (win < 0) or (win >= NW)
                if pad:
                    dlo, dhi = 0, 3
                else:
                    dlo, dhi = max(0, c - 8), min(3, c)
                    if dlo > dhi:
                        continue

                def smm(tensor, wa=wa, h=h, c=c, pad=pad, win=win, fcH=fcH, hh=hh):
                    bank_wait(tensor, c % 2)
                    if pad:
                        lhs = kpad[hh * 64:hh * 64 + 64, :]
                    else:
                        slot = win % 3
                        lhs = kroll[hh * 64:hh * 64 + 64,
                                    (slot * 8 + fcH) * WS + (c % 4) * 128:
                                    (slot * 8 + fcH) * WS + (c % 4 + 1) * 128]
                    tensor.matmul(pb[c % 2][:, :], lhs,
                                  qbuf[hh * 64:hh * 64 + 64, fcH * WS:(fcH + 1) * WS],
                                  start=True, stop=True).then_inc(sT, 1)
                    cT.inc()
                T(smm)
                smv = cT.v

                tri_need = []
                if not pad:
                    if dlo <= c <= dhi:
                        tri_need.append((c, tri_l))
                    if dlo <= c - 8 <= dhi:
                        tri_need.append((c - 8, tri_u))
                mv = None
                if tri_need:
                    def tri(vector, c=c, tri_need=tri_need, smv=smv):
                        vector.wait_ge(sT, smv)
                        for (d, tmask) in tri_need:
                            vector.tensor_tensor(pb[c % 2][:, d * 128:(d + 1) * 128],
                                                 pb[c % 2][:, d * 128:(d + 1) * 128],
                                                 tmask[:, :], ALU.add)
                        vector.drain()
                        vector.tensor_copy(e2_t[:, :], czero[:, :]).then_inc(sV, 1)
                        cV.inc()
                    Vv(tri)
                    mv = cV.v

                def expc(scalar, c=c, dlo=dlo, dhi=dhi, smv=smv, mv=mv):
                    if mv is not None:
                        scalar.wait_ge(sV, mv)
                    else:
                        scalar.wait_ge(sT, smv)
                    if attn_last_av[c % 2]:
                        scalar.wait_ge(sT, attn_last_av[c % 2])
                    scalar.activation(
                        out=attnp[:, (c % 2) * WS + dlo * 128:(c % 2) * WS + (dhi + 1) * 128],
                        in_=pb[c % 2][:, dlo * 128:(dhi + 1) * 128],
                        func=AF.Exp).then_inc(sS, 1)
                    cS.inc()
                Sc(expc)
                expv = cS.v
                bank_s[c % 2] = expv

                def avc(tensor, h=h, c=c, dlo=dlo, dhi=dhi, expv=expv, pad=pad, win=win,
                        av_started=av_started):
                    tensor.wait_ge(sS, expv)
                    for d in range(dlo, dhi + 1):
                        if not av_started[d]:
                            bank_wait(tensor, 2 + d)
                        if pad:
                            rhs = vpad[:, h * 65:(h + 1) * 65]
                        else:
                            slot = win % 3
                            base = (slot * 4 + c % 4) * 16 * 65
                            rhs = vroll[:, base + h * 65: base + (h + 1) * 65]
                        tensor.matmul(pb[2 + d][:, 0:65],
                                      attnp[:, (c % 2) * WS + d * 128:(c % 2) * WS + (d + 1) * 128],
                                      rhs, start=(not av_started[d]), stop=(c == lastc[d]))
                        av_started[d] = True
                    pe_bump(tensor)
                T(avc)
                attn_last_av[c % 2] = cT.v
            av_head_done = cT.v

            def avev(vector, h=h, av_head_done=av_head_done, wa=wa):
                vector.wait_ge(sT, av_head_done)
                if debug and wa == 1 and h == 0:
                    vector.wait_ge(qP, 80)
                for d in range(4):
                    vector.reciprocal(rz_t[:, :], pb[2 + d][:, 64:65])
                    vector.drain()
                    vector.tensor_scalar(attnout[:, d * E + h * 64:d * E + (h + 1) * 64],
                                         pb[2 + d][:, 0:64], rz_t[:, :], None, ALU.mult)
                    vector.drain()
                vector.tensor_copy(rz_t[:, :], czero[:, :]).then_inc(sV, 1)
                cV.inc()
            Vv(avev)
            for d in range(4):
                bank_v[2 + d] = cV.v
        return cV.v

    def emit_tail(wa, attend_done_v):
        slot = wa % 2
        for tt in range(4):
            def tp(tensor, tt=tt, attend_done_v=attend_done_v):
                if tt == 0:
                    tensor.wait_ge(sV, attend_done_v)
                if aoT_guard[0]:
                    tensor.wait_ge(sV, aoT_guard[0])
                bank_wait(tensor, 7)
                for fc in range(8):
                    tensor.transpose(pbf[:, fc * 128:(fc + 1) * 128],
                                     attnout[:, tt * E + fc * 128:tt * E + (fc + 1) * 128],
                                     id_bf[:, :])
                pe_bump(tensor)
            T(tp)
            tpv = cT.v

            def tev(vector, tpv=tpv):
                vector.wait_ge(sT, tpv)
                vector.tensor_copy(aoT[:, :], pbf[:, :]).then_inc(sV, 1)
                cV.inc()
            Vv(tev)
            bank_v[7] = cV.v
            aov = cV.v

            for half in range(2):
                bank = (tt * 2 + half) % 2

                def omm(tensor, tt=tt, half=half, wa=wa, aov=aov, bank=bank):
                    tensor.wait_ge(sV, aov)
                    if tt == 0 and half == 0:
                        tensor.wait_ge(qW[WO], wload_marks[(min(wa + 1, NW - 1), WO)])
                        tensor.wait_ge(sS, cS.v)
                    bank_wait(tensor, bank)
                    for fc in range(8):
                        tensor.matmul(pb[bank][:, :],
                                      aoT[:, fc * 128:(fc + 1) * 128],
                                      wslot_ap(1)[:, fc * E + half * 512: fc * E + (half + 1) * 512],
                                      start=(fc == 0), stop=(fc == 7))
                    pe_bump(tensor)
                T(omm)
                omv = cT.v

                def oev(vector, tt=tt, half=half, wa=wa, omv=omv, slot=slot, bank=bank):
                    vector.wait_ge(sT, omv)
                    xt = xw[:, (slot * 4 + tt) * E + half * 512:(slot * 4 + tt) * E + (half + 1) * 512]
                    vector.tensor_tensor(pb[bank][:, :], pb[bank][:, :],
                                         boTb[:, half * 512:(half + 1) * 512], ALU.add)
                    vector.drain()
                    vector.tensor_tensor(pb[bank][:, :], pb[bank][:, :],
                                         gaTb[:, half * 512:(half + 1) * 512], ALU.mult)
                    vector.drain()
                    vector.tensor_tensor(xt, xt, pb[bank][:, :], ALU.add)
                    vector.drain()
                    vector.tensor_copy(rz_t[:, :], czero[:, :]).then_inc(sV, 1)
                    cV.inc()
                Vv(oev)
                bank_v[bank] = cV.v
            x1_done = cV.v

            def x1st(scalar, wa=wa, tt=tt, x1_done=x1_done, slot=slot):
                scalar.wait_ge(sV, x1_done)
                scalar.dma_start(out=out_h[wa * WS + tt * 128: wa * WS + (tt + 1) * 128, :],
                                 in_=xw[:, (slot * 4 + tt) * E:(slot * 4 + tt + 1) * E]
                                 ).then_inc(qO, 16)
                cQO.inc(16)
            Sc(x1st)
            x1_written[wa * 4 + tt] = cQO.v

            xt_full = xw[:, (slot * 4 + tt) * E:(slot * 4 + tt + 1) * E]

            def stats2(vector, xt=xt_full, x1_done=x1_done):
                vector.wait_ge(sV, x1_done)
                vector.tensor_reduce(mean_t[:, :], xt, axis=AX.X, op=ALU.add, negate=True)
                vector.tensor_tensor(sq_t[:, :], xt, xt, ALU.mult)
                vector.drain()
                vector.tensor_reduce(var_t[:, :], sq_t[:, :], axis=AX.X, op=ALU.add)
                vector.drain()
                vector.tensor_scalar(mean_t[:, :], mean_t[:, :], crecE[:, :], None, ALU.mult)
                vector.tensor_scalar(var_t[:, :], var_t[:, :], crecE[:, :], None, ALU.mult)
                vector.drain()
                vector.tensor_tensor(sq_t[:, 0:1], mean_t[:, :], mean_t[:, :], ALU.mult)
                vector.drain()
                vector.tensor_tensor(var_t[:, :], var_t[:, :], sq_t[:, 0:1], ALU.subtract).then_inc(sV, 1)
                cV.inc()
            Vv(stats2)
            stv = cV.v

            def rstd2(scalar, stv=stv):
                scalar.wait_ge(sV, stv)
                scalar.activation(out=rstd_t[:, :], in_=var_t[:, :], func=AF.Sqrt,
                                  bias=ceps[:, :]).then_inc(sS, 1)
                cS.inc()
            Sc(rstd2)
            rsv = cS.v

            def norm2(vector, xt=xt_full, tt=tt, rsv=rsv, wa=wa):
                vector.wait_ge(sS, rsv)
                if debug and wa == 0 and tt == 0:
                    vector.wait_ge(qP, 80)
                g = h1t_guard.get(tt % 2)
                if g:
                    vector.wait_ge(sT, g)
                g2 = h2m_norm_guard.get(tt % 2)
                if g2:
                    vector.wait_ge(sG, g2)
                vector.reciprocal(rstd_t[:, :], rstd_t[:, :])
                vector.drain()
                vector.tensor_scalar(h1t[:, (tt % 2) * E:(tt % 2 + 1) * E], xt,
                                     mean_t[:, :], rstd_t[:, :],
                                     ALU.add, ALU.mult).then_inc(sV, 1)
                cV.inc()
            Vv(norm2)
            nv = cV.v

            def h2m_g(gpsimd, tt=tt, nv=nv):
                gpsimd.wait_ge(sV, nv)
                g = h2mt_dma_guard.get(tt % 2)
                if g:
                    gpsimd.wait_ge(qH, g)
                n = h1t[:, (tt % 2) * E:(tt % 2 + 1) * E]
                dst = h2mt[:, (tt % 2) * E:(tt % 2 + 1) * E]
                gpsimd.tensor_tensor(dst, n, cbTb[:, :], ALU.mult)
                gpsimd.drain()
                gpsimd.tensor_tensor(dst, dst, sbTb[:, :], ALU.add)
                gpsimd.drain()
                gpsimd.tensor_tensor(dst, dst, n, ALU.add).then_inc(sG, 1)
                cG.inc()
            Gp(h2m_g)
            h2gv = cG.v
            h2m_norm_guard[tt % 2] = h2gv
            h1tg_g[tt % 2] = h2gv

            def h2mst(scalar, wa=wa, tt=tt, h2gv=h2gv, prevq=cQH.v):
                scalar.wait_ge(sG, h2gv)
                if prevq:
                    scalar.wait_ge(qH, prevq)
                scalar.dma_start(out=h2m_hbm[wa * WS + tt * 128: wa * WS + (tt + 1) * 128, :],
                                 in_=h2mt[:, (tt % 2) * E:(tt % 2 + 1) * E]).then_inc(qH, 16)
                cQH.inc(16)
            Sc(h2mst)
            h2m_written[wa * 4 + tt] = cQH.v
            h2mt_dma_guard[tt % 2] = cQH.v

            def rtp(tensor, tt=tt, h2gv=h2gv):
                tensor.wait_ge(sG, h2gv)
                if aoT_guard[0]:
                    tensor.wait_ge(sV, aoT_guard[0])
                bank_wait(tensor, 7)
                for fc in range(8):
                    tensor.transpose(pbf[:, fc * 128:(fc + 1) * 128],
                                     h2mt[:, (tt % 2) * E + fc * 128:(tt % 2) * E + (fc + 1) * 128],
                                     id_bf[:, :])
                pe_bump(tensor)
            T(rtp)
            rtv = cT.v

            def rtev(vector, rtv=rtv):
                vector.wait_ge(sT, rtv)
                vector.tensor_copy(aoT[:, :], pbf[:, :]).then_inc(sV, 1)
                cV.inc()
            Vv(rtev)
            bank_v[7] = cV.v
            rtev_v = cV.v
            aoT_guard[0] = rtev_v

            def logit(tensor, rtev_v=rtev_v):
                tensor.wait_ge(sV, rtev_v)
                bank_wait(tensor, 4)
                tensor.matmul(pb[4][:, 0:8], ones_bf[:, :], bs_bf[:, :],
                              start=True, stop=False)
                for fc in range(8):
                    tensor.matmul(pb[4][:, 0:8], aoT[:, fc * 128:(fc + 1) * 128],
                                  Wsb[:, fc * 8:(fc + 1) * 8],
                                  start=False, stop=(fc == 7))
                pe_bump(tensor)
            T(logit)
            lgv = cT.v

            def sfm1(scalar, lgv=lgv):
                scalar.wait_ge(sT, lgv)
                if probe_last_ptv[0]:
                    scalar.wait_ge(sT, probe_last_ptv[0])
                scalar.activation(out=probe_tok[:, :], in_=pb[4][:, 0:8],
                                  func=AF.Exp).then_inc(sS, 1)
                cS.inc()
            Sc(sfm1)
            sf1 = cS.v
            bank_s[4] = sf1

            def sfm2(vector, sf1=sf1):
                vector.wait_ge(sS, sf1)
                vector.tensor_reduce(rz_t[:, :], probe_tok[:, :], axis=AX.X, op=ALU.add)
                vector.drain()
                vector.reciprocal(rz_t[:, :], rz_t[:, :])
                vector.drain()
                vector.tensor_scalar(probe_tok[:, :], probe_tok[:, :], rz_t[:, :],
                                     None, ALU.mult)
                vector.drain()
                vector.tensor_copy(e1_t[:, :], czero[:, :]).then_inc(sV, 1)
                cV.inc()
            Vv(sfm2)
            sf2 = cV.v

            def ptp(tensor, sf2=sf2):
                tensor.wait_ge(sV, sf2)
                bank_wait(tensor, 5)
                tensor.transpose(pb[5][0:8, 0:128], probe_tok[:, :], id_f32[:, :])
                pe_bump(tensor)
            T(ptp)
            ptv = cT.v
            probe_last_ptv[0] = ptv

            def pev(vector, wa=wa, tt=tt, ptv=ptv, prevq=probs_sent[0]):
                vector.wait_ge(sT, ptv)
                if prevq:
                    vector.wait_ge(qP, prevq)
                vector.tensor_copy(pstage[:, :], pb[5][0:8, 0:128]).then_inc(sV, 1)
                cV.inc()
            Vv(pev)
            bank_v[5] = cV.v
            pevv = cV.v

            def pst(scalar, wa=wa, tt=tt, pevv=pevv):
                scalar.wait_ge(sV, pevv)
                scalar.dma_start(out=cin_hbm[:, wa * WS + tt * 128: wa * WS + (tt + 1) * 128],
                                 in_=pstage[:, :]).then_inc(qP, 16)
                cQP.inc(16)
            Sc(pst)
            probs_sent[0] = cQP.v

    aoT_guard = [0]
    probe_last_ptv = [0]
    probs_sent = [0]
    h2m_norm_guard = {}
    h2mt_dma_guard = {}

    for it in range(NW + 1):
        w, wa = it, it - 1
        if w < NW:
            pass
            emit_h1(w)
            if debug and w == 0:
                def d1(scalar, g=cV.v):
                    scalar.wait_ge(sV, g)
                    scalar.dma_start(out=dbg_h1T[:, :], in_=h1T[:, :]).then_inc(qP, 16)
                    cQP.inc(16)
                Sc(d1)
            emit_proj_K(w)

            def ldq(sync, w=w):
                load_w(sync, w, WQ, 0, cT.v)
            Sy(ldq)
            if debug and w == 0:
                def d2(scalar, g=cV.v):
                    scalar.wait_ge(sV, g)
                    scalar.dma_start(out=dbg_k[:, :], in_=kroll[:, 0:8 * WS]).then_inc(qP, 16)
                    cQP.inc(16)
                Sc(d2)
            emit_proj_V(w)

            def ldo(sync, w=w):
                load_w(sync, w, WO, 1, cT.v)
            Sy(ldo)
        adv = None
        if debug and w == 0:
            def d3(scalar, g=cV.v):
                scalar.wait_ge(sV, g)
                scalar.dma_start(out=dbg_v[:, :], in_=vroll[:, 0:4 * 16 * 65]).then_inc(qP, 16)
                cQP.inc(16)
            Sc(d3)
        if wa >= 0:
            adv = emit_attend(wa)
        if debug and wa == 0:
            def d5(scalar, g=adv):
                scalar.wait_ge(sV, g)
                scalar.dma_start(out=dbg_ao[:, :], in_=attnout[:, :]).then_inc(qP, 16)
                cQP.inc(16)
            Sc(d5)
        if w < NW:
            emit_proj_Q(w)
        if debug and w == 0:
            def d4(scalar, g=cV.v):
                scalar.wait_ge(sV, g)
                scalar.dma_start(out=dbg_q[:, :], in_=qbuf[:, :]).then_inc(qP, 16)
                cQP.inc(16)
            Sc(d4)
        if wa >= 0:
            emit_tail(wa, adv)
        if w + 1 < NW:
            gates = (cV.v, cS.v, cQO.v)

            def xl(scalar, w=w, gates=gates):
                scalar.wait_ge(sV, gates[0])
                scalar.wait_ge(sS, gates[1])
                scalar.wait_ge(qO, gates[2])
                scalar.dma_start(
                    out=xw[:, ((w + 1) % 2) * 4 * E:(((w + 1) % 2) + 1) * 4 * E].rearrange(
                        "p (t f) -> p t f", t=4),
                    in_=x_h[(w + 1) * WS:(w + 2) * WS, :].rearrange(
                        "(t p) f -> p t f", p=128)).then_inc(qX, 16)
                cQX.inc(16)
            Sc(xl)
            xload_marks[w + 1] = cQX.v
        if w < NW and w + 1 < NW:
            def ldk(sync, w=w):
                load_w(sync, w + 1, WK, 0, cT.v)
            Sy(ldk)

            def ldv2(sync, w=w):
                load_w(sync, w + 1, WV, 1, cT.v)
            Sy(ldv2)

    probsT_done = cV.v
    x1_all_w = cQO.v
    h2m_all_w = cQH.v
    send_done = cQP.v

    if sim_single:
        def p2_coll(scalar):
            scalar.wait_ge(qP, send_done)
            for cc in range(8):
                scalar.dma_start(out=cout_hbm[cc * 8:(cc + 1) * 8, :],
                                 in_=cin_hbm[:, :]).then_inc(qP, 16)
                cQP.inc(16)
        Sc(p2_coll)
        cd = cQP.v

        def p2_collw(gpsimd):
            gpsimd.wait_ge(qP, cd)
            gpsimd.memset(nfb_t[0:1, 0:1], 0.0).then_inc(csem, 1)
        Gp(p2_collw)
    else:
        def p2_coll(gpsimd):
            gpsimd.wait_ge(qP, send_done)
            gpsimd.collective_compute(
                "AllGather", ALU.bypass,
                replica_groups=[list(range(8))],
                ins=[cin_hbm[:, :]], outs=[cout_hbm[:, :]],
            ).then_inc(csem, 1)
        Gp(p2_coll)

    def p2_fetch(sync):
        sync.wait_ge(csem, 1)
        sync.dma_start(out=search[:, :],
                       in_=cout_hbm[:, :].rearrange("r (h f) -> (r h) f", h=2)).then_inc(qS, 16)
        cQS.inc(16)
    Sy(p2_fetch)
    search_done = cQS.v

    bis_t = []
    for itn in range(NITER + 1):
        def upd(vector, itn=itn):
            if itn == 0:
                vector.wait_ge(qS, search_done)
                vector.wait_ge(qO, x1_all_w)
                vector.wait_ge(qH, h2m_all_w)
            else:
                vector.wait_ge(sT, bis_t[-1])
                vector.tensor_scalar(ge_t[:, :], pb[3][:, 0:1], ck[:, :], None, ALU.is_ge)
                vector.drain()
                vector.tensor_tensor(e1_t[:, :], tm_t[:, :], lo_t[:, :], ALU.subtract)
                vector.tensor_tensor(e2_t[:, :], tm_t[:, :], hi_t[:, :], ALU.subtract)
                vector.drain()
                vector.tensor_tensor(e1_t[:, :], ge_t[:, :], e1_t[:, :], ALU.mult)
                vector.tensor_tensor(e2_t[:, :], ge_t[:, :], e2_t[:, :], ALU.mult)
                vector.drain()
                vector.tensor_tensor(lo_t[:, :], lo_t[:, :], e1_t[:, :], ALU.add)
                vector.tensor_tensor(hi_t[:, :], tm_t[:, :], e2_t[:, :], ALU.subtract)
                vector.drain()
                vector.tensor_tensor(tm_t[:, :], lo_t[:, :], hi_t[:, :], ALU.add)
                vector.drain()
                vector.tensor_scalar(tm_t[:, :], tm_t[:, :], chalf[:, :], None, ALU.mult)
                vector.drain()
            if itn < NITER:
                vector.tensor_scalar(bisdump[:, :], search[:, :], tm_t[:, :],
                                     None, ALU.is_gt, ALU.add, accum_out=cnt_t[:, :]).then_inc(sV, 1)
            else:
                vector.tensor_copy(e1_t[:, :], lo_t[:, :]).then_inc(sV, 1)
            cV.inc()
        Vv(upd)
        cntv = cV.v
        if itn > 0:
            bank_v[3] = cntv
        if itn == NITER:
            break

        def red(tensor, cntv=cntv):
            tensor.wait_ge(sV, cntv)
            bank_wait(tensor, 2)
            tensor.matmul(pb[2][0:8, 0:1], selm[:, :], cnt_t[:, :], start=True, stop=True)
            pe_bump(tensor)
        T(red)
        rv_ = cT.v

        def red2(vector, rv_=rv_):
            vector.wait_ge(sT, rv_)
            vector.tensor_copy(cntE[:, :], pb[2][0:8, 0:1]).then_inc(sV, 1)
            cV.inc()
        Vv(red2)
        bank_v[2] = cV.v
        r2v = cV.v

        def red3(tensor, r2v=r2v):
            tensor.wait_ge(sV, r2v)
            bank_wait(tensor, 3)
            tensor.matmul(pb[3][:, 0:1], selmT[:, :], cntE[:, :], start=True, stop=True)
            pe_bump(tensor)
        T(red3)
        bis_t.append(cT.v)
    bisect_done = cV.v

    extm_built = [0]

    def p2_ext(gpsimd):
        gpsimd.iota(ids16[:, :], pattern=[[16, SB]], base=1, channel_multiplier=1,
                    allow_small_or_imprecise_dtypes=True)
        gpsimd.drain()
        gpsimd.tensor_copy(nfb_t[:, :], nfb_t[:, :]).then_inc(sG, 1)
        cG.inc()
    Gp(p2_ext)
    ids_done = cG.v

    def p2_thr(tensor):
        tensor.wait_ge(sV, bisect_done)
        bank_wait(tensor, 2)
        tensor.matmul(pb[2][0:8, 0:1], selm[:, :], lo_t[:, :], start=True, stop=True)
        pe_bump(tensor)
    T(p2_thr)
    thrv = cT.v

    def p2_thr2(vector):
        vector.wait_ge(sT, thrv)
        vector.tensor_scalar(thr8[:, :], pb[2][0:8, 0:1], c16th[:, :], None, ALU.mult)
        vector.drain()
        vector.tensor_copy(e1_t[:, :], czero[:, :]).then_inc(sV, 1)
        cV.inc()
    Vv(p2_thr2)
    bank_v[2] = cV.v
    thr_done = cV.v

    def p3_pload(sync):
        sync.wait_ge(sV, thr_done)
        sync.dma_start(out=probs_sb[:, :], in_=cin_hbm[:, :]).then_inc(qF, 16)
        cQF.inc(16)
    Sy(p3_pload)
    pl_done = cQF.v

    def p3_sel(vector):
        vector.wait_ge(qF, pl_done)
        vector.tensor_scalar(selT[:, :], probs_sb[:, :], thr8[:, :], None, ALU.is_gt).then_inc(sV, 1)
        cV.inc()
    Vv(p3_sel)
    selv = cV.v

    def p3_iota(gpsimd):
        gpsimd.wait_ge(sV, selv)
        gpsimd.iota(vTt[:, :], pattern=[[1, SEQ]], base=1, channel_multiplier=0,
                    allow_small_or_imprecise_dtypes=True)
        gpsimd.drain()
        gpsimd.tensor_copy(nfb_t[:, :], nfb_t[:, :]).then_inc(sG, 1)
        cG.inc()
    Gp(p3_iota)
    iotav = cG.v

    def p3_vt(vector):
        vector.wait_ge(sG, iotav)
        vector.scalar_tensor_tensor(vTt[:, :], probs_sb[:, :], chalf8[:, :], vTt[:, :],
                                    ALU.mult, ALU.add)
        vector.drain()
        vector.tensor_tensor(vTt[:, :], vTt[:, :], selT[:, :], ALU.mult)
        vector.drain()
        vector.tensor_scalar(vTt[:, :], vTt[:, :], cone8[:, :], None, ALU.subtract).then_inc(sV, 1)
        cV.inc()
    Vv(p3_vt)
    vtv = cV.v

    def p3_vtst(scalar):
        scalar.wait_ge(sV, vtv)
        scalar.dma_start(out=vT_hbm[:, :], in_=vTt[:, :]).then_inc(qVT, 16)
        cQVT.inc(16)
    Sc(p3_vtst)
    vt_hbm_done = cQVT.v

    NBLK = 4
    F1B = 8
    STN = [512, 128]
    h2m_all_written = h2m_written[(NW - 1) * 4 + 3]
    x1_all_written = x1_written[(NW - 1) * 4 + 3]
    wblk_marks = {}

    def load_wblk(gp, e, blk, wait_t):
        par = blk % 2
        if wait_t:
            gp.wait_ge(sT, wait_t)
        if cGQ2.v and wait_t:
            gp.wait_ge(gq2, cGQ2.v)
            gp.wait_ge(gq3, cGQ3.v)
        gp.dma_start(out=W1blk[:, par * 8 * KB:(par + 1) * 8 * KB].rearrange("p (d f) -> p d f", d=8),
                     in_=W1_h[e, :, blk * KB:(blk + 1) * KB].rearrange("(d p) f -> p d f", p=128)
                     ).then_inc(gq2, 16)
        cGQ2.inc(16)
        gp.dma_start(out=W2blk[:, par * 8 * KB:(par + 1) * 8 * KB].rearrange("p (f d) -> p f d", f=8),
                     in_=W2_h[e, blk * KB:(blk + 1) * KB, :].rearrange("(f p) d -> p f d", p=128)
                     ).then_inc(gq3, 16)
        cGQ3.inc(16)
        wblk_marks[(e, blk)] = (cGQ2.v, cGQ3.v)

    def p3_wblk0(gpsimd):
        gpsimd.wait_ge(sV, vtv)
        load_wblk(gpsimd, 0, 0, None)
        load_wblk(gpsimd, 0, 1, None)
        wblk_marks[(0, 0)] = wblk_marks[(0, 1)]
    Gp(p3_wblk0)

    gather_marks = {}
    compact_v = {}
    hexp_ready = {}
    rv_ready = {}
    b2_used = [0]
    b2_vused = [0]
    mm2_last_by_par = {}

    for e in range(NEXP):
        def cfetch(sync, e=e):
            sync.wait_ge(qVT, vt_hbm_done)
            if e > 0:
                sync.wait_ge(sG, compact_v[("sg", e - 1)])
            sync.dma_start(out=v16[:, :],
                           in_=vT_hbm[e:e + 1, :].rearrange("r (b a) -> (r a) b", a=16),
                           allow_slow_non_contiguous=True).then_inc(qF, 16)
            cQF.inc(16)
        Sy(cfetch)
        cfv = cQF.v

        def sg(gpsimd, e=e, cfv=cfv):
            gpsimd.wait_ge(qF, cfv)
            gpsimd.wait_ge(sG, ids_done)
            if e > 0:
                gpsimd.wait_ge(sV, compact_v[("rv", e - 1)])
            gpsimd.sparse_gather(cmp16[:, :], v16[:, :], num_found=nf_t[:, :])
            gpsimd.drain()
            gpsimd.tensor_copy(nff_t[:, :], nf_t[:, :])
            gpsimd.drain()
            gpsimd.partition_broadcast(nfb_t[:, :], nff_t[:, :], channels=16).then_inc(sG, 1)
            cG.inc()
        Gp(sg)
        compact_v[("sg", e)] = cG.v
        sgv = cG.v

        def csel(vector, e=e, sgv=sgv):
            vector.wait_ge(sG, sgv)
            if e > 0:
                vector.wait_ge(qC, compact_v[("idx", e - 1)])
            vector.tensor_scalar(m16[:, :], ids16[:, :], nfb_t[:, :], None, ALU.is_le)
            vector.drain()
            vector.select(clean16[:, :], m16[:, :], cmp16[:, :], zz16[:, :], add_drain=True)
            vector.drain()
            vector.tensor_copy(idxlo[:, :], clean16[:, 0:CB])
            vector.drain()
            vector.tensor_copy(idflo[:, :], idxlo[:, :])
            vector.drain()
            vector.tensor_tensor(rv16[:, :], clean16[:, 0:CB], idflo[:, :], ALU.subtract)
            vector.drain()
            vector.tensor_scalar(rv16[:, :], rv16[:, :], ctwo[0:16, :], None, ALU.mult).then_inc(sV, 1)
            cV.inc()
        Vv(csel)
        compact_v[("rv", e)] = cV.v
        cselv = cV.v

        def cidx(sync, e=e, cselv=cselv):
            sync.wait_ge(sV, cselv)
            sync.dma_start(out=idx_all[0:16, e * CB:(e + 1) * CB], in_=idxlo[:, :]).then_inc(qC, 16)
            sync.dma_start(out=idx_all[16:32, e * CB:(e + 1) * CB], in_=idxlo[:, :]).then_inc(qC, 16)
            sync.dma_start(out=rv_hbm[e, :].rearrange("(b a) -> a b", a=16), in_=rv16[:, :],
                           allow_slow_non_contiguous=True).then_inc(qC, 16)
            cQC.inc(48)
            sync.wait_ge(qC, cQC.v)
            sync.dma_start(out=rvsm_all[:, e * (C // 128):(e + 1) * (C // 128)],
                           in_=rv_hbm[e, :].rearrange("(q p) -> p q", p=128),
                           allow_slow_non_contiguous=True).then_inc(qC, 16)
            cQC.inc(16)
        Sy(cidx)
        cidxv = cQC.v
        compact_v[("idx", e)] = cidxv
        rv_ready[e] = cidxv

        def gat(gpsimd, e=e, cidxv=cidxv):
            gpsimd.wait_ge(qC, cidxv)
            gpsimd.wait_ge(qH, h2m_all_written)
            gpsimd.wait_ge(qO, x1_all_written)
            if e >= 2:
                gpsimd.wait_ge(sT, gather_marks[("used", e - 2)])
            gpsimd.dma_gather(
                xinT[:, (e % 2) * 8 * C:((e % 2) + 1) * 8 * C].rearrange("p (a i) -> p a i", a=8),
                h2m_hbm[:, :], idx_all[:, e * CB:(e + 1) * CB],
                num_idxs=C, num_idxs_reg=C, elem_size=E,
                elem_step=h2m_hbm.ap[0][0], transpose=True,
            ).then_inc(gsems[e], 16)
        Gp(gat)

        def b2ld(gpsimd, e=e):
            if e > 0:
                gpsimd.wait_ge(sG, b2_used[0])
            gpsimd.dma_start(out=b2row_bf[:, :], in_=b2r_h[e:e + 1, :]).then_inc(gq0, 16)
            cGQ0.inc(16)
        Gp(b2ld)
        b2lv = cGQ0.v
        b2_used[0] = cG.v + 1

        def b2bc(gpsimd, e=e, b2lv=b2lv):
            gpsimd.wait_ge(gq0, b2lv)
            if e > 0:
                gpsimd.wait_ge(sV, b2_vused[0])
            gpsimd.partition_broadcast(b2Tsb[:, :], b2row_bf[0:1, :], channels=128)
            gpsimd.drain()
            gpsimd.memset(nfb_t[0:1, 0:1], 0.0).then_inc(sG, 1)
            cG.inc()
        Gp(b2bc)
        b2done_g = cG.v
        b2_used[0] = b2done_g

        def b2fin(vector, b2done_g=b2done_g):
            vector.wait_ge(sG, b2done_g)
            vector.tensor_copy(e1_t[:, :], czero[:, :]).then_inc(sV, 1)
            cV.inc()
        Vv(b2fin)
        b2done = cV.v

        for blk in range(NBLK):
            par = blk % 2
            if not (e == 0 and blk <= 1):
                pass
            for f1i in range(F1B):
                f1t = blk * F1B + f1i

                def mm1(tensor, e=e, blk=blk, f1i=f1i, f1t=f1t, par=par):
                    if f1i == 0:
                        tensor.wait_ge(gq2, wblk_marks[(e, blk)][0])
                        tensor.wait_ge(gq3, wblk_marks[(e, blk)][1])
                        tensor.wait_ge(gsems[e], 16)
                    for sg_i, (off, n) in enumerate([(0, 512), (512, 128)]):
                        bank_wait(tensor, sg_i)
                        for d in range(8):
                            tensor.matmul(
                                pb[sg_i][:, 0:n],
                                W1blk[:, par * 8 * KB + d * KB + f1i * 128:
                                      par * 8 * KB + d * KB + (f1i + 1) * 128],
                                xinT[:, (e % 2) * 8 * C + d * C + off:
                                     (e % 2) * 8 * C + d * C + off + n],
                                start=(d == 0), stop=(d == 7))
                    pe_bump(tensor)
                T(mm1)
                mm1v = cT.v
                if f1i == F1B - 1 and blk == NBLK - 1:
                    gather_marks[("used", e)] = mm1v

                def gel(scalar, e=e, f1t=f1t, f1i=f1i, mm1v=mm1v, blk=blk):
                    scalar.wait_ge(sT, mm1v)
                    hb = (e % 2) * 8 * C
                    scalar.activation(out=hexp[:, hb + f1i * C: hb + f1i * C + 512],
                                      in_=pb[0][:, 0:512], func=AF.Gelu_apprx_tanh,
                                      bias=b1c[:, e * 32 + f1t:e * 32 + f1t + 1])
                    scalar.activation(out=hexp[:, hb + f1i * C + 512: hb + f1i * C + 640],
                                      in_=pb[1][:, 0:128], func=AF.Gelu_apprx_tanh,
                                      bias=b1c[:, e * 32 + f1t:e * 32 + f1t + 1]).then_inc(sS, 1)
                    cS.inc()
                Sc(gel)
                bank_s[0] = cS.v
                bank_s[1] = cS.v
                gelv = cS.v
                hexp_ready[(e, blk, f1i)] = gelv

            mm2banks = (2, 3) if blk % 2 == 0 else (4, 5)
            for st in range(5):
                def mm2(tensor, e=e, blk=blk, st=st, par=par, mm2banks=mm2banks):
                    tensor.wait_ge(sS, hexp_ready[(e, blk, F1B - 1)])
                    bank_wait(tensor, mm2banks[0])
                    bank_wait(tensor, mm2banks[1])
                    hb = (e % 2) * 8 * C
                    for f1i in range(F1B):
                        for half in range(2):
                            tensor.matmul(
                                pb[mm2banks[half]][:, :],
                                hexp[:, hb + f1i * C + st * 128: hb + f1i * C + (st + 1) * 128],
                                W2blk[:, par * 8 * KB + f1i * KB + half * 512:
                                      par * 8 * KB + f1i * KB + (half + 1) * 512],
                                start=(f1i == 0), stop=(f1i == F1B - 1))
                    pe_bump(tensor)
                T(mm2)
                mm2v = cT.v
                mm2_last_by_par[par] = mm2v

                def oadd(vector, e=e, blk=blk, st=st, mm2v=mm2v, mm2banks=mm2banks, b2done=b2done):
                    vector.wait_ge(sT, mm2v)
                    if blk == 0:
                        vector.wait_ge(sV, b2done)
                        if e > 0:
                            vector.wait_ge(ssems[e - 1], 16)
                    for half in range(2):
                        dst = outacc[:, st * E + half * 512:st * E + (half + 1) * 512]
                        if blk == 0:
                            vector.tensor_tensor(dst, pb[mm2banks[half]][:, :],
                                                 b2Tsb[:, half * 512:(half + 1) * 512], ALU.add)
                        else:
                            vector.tensor_tensor(dst, dst, pb[mm2banks[half]][:, :], ALU.add)
                    vector.drain()
                    if blk == NBLK - 1:
                        vector.wait_ge(qC, rv_ready[e])
                        vector.tensor_scalar(outacc[:, st * E:(st + 1) * E],
                                             outacc[:, st * E:(st + 1) * E],
                                             rvsm_all[:, e * 5 + st:e * 5 + st + 1],
                                             None, ALU.mult)
                        vector.drain()
                        vector.tensor_tensor(outacc[:, st * E:(st + 1) * E],
                                             outacc[:, st * E:(st + 1) * E],
                                             gbTb[:, :], ALU.mult)
                        vector.drain()
                    vector.tensor_copy(e2_t[:, :], czero[:, :]).then_inc(sV, 1)
                    cV.inc()
                Vv(oadd)
                bank_v[mm2banks[0]] = cV.v
                bank_v[mm2banks[1]] = cV.v
                if blk == 0:
                    b2_vused[0] = cV.v
            def ldnext(gpsimd, e=e, blk=blk, gate=cT.v):
                ee, bb = e, blk + 2
                if bb >= NBLK:
                    ee, bb = e + 1, bb - NBLK
                if ee < NEXP:
                    load_wblk(gpsimd, ee, bb, gate)
            Gp(ldnext)
        scale_done = cV.v

        def scat(gpsimd, e=e, scale_done=scale_done):
            gpsimd.wait_ge(sV, scale_done)
            gpsimd.wait_ge(qO, x1_all_written)
            if e > 0:
                gpsimd.wait_ge(ssems[e - 1], 16)
            gpsimd.dma_scatter_add(
                out_h[:, :], outacc[:, :].rearrange("p (a d) -> p a d", a=5),
                idx_all[:, e * CB:(e + 1) * CB],
                num_idxs=C, num_idxs_reg=C, elem_size=E,
                elem_step=out_h.ap[0][0],
            ).then_inc(ssems[e], 16)
        Gp(scat)

    def fin(gpsimd):
        for e in range(NEXP):
            gpsimd.wait_ge(ssems[e], 16)
        gpsimd.wait_ge(qO, x1_all_written)
    Gp(fin)

    def fin2(scalar):
        scalar.wait_ge(ssems[NEXP - 1], 16)
    Sc(fin2)

    es.close()
    return nc


_NC_CACHE = {}


def _get_nc():
    if "nc" not in _NC_CACHE:
        nc = build_nc()
        nc.compile()
        _NC_CACHE["nc"] = nc
    return _NC_CACHE["nc"]


def kernel(L0_attn_x, L0_attn_wq, L0_attn_bq, L0_attn_wk, L0_attn_bk,
           L0_attn_wv, L0_attn_bv, L0_attn_wo, L0_attn_bo,
           L0_ffn_Ws, L0_ffn_bs, L0_ffn_W1, L0_ffn_b1, L0_ffn_W2, L0_ffn_b2,
           L0_c, L0_wmod):
    bf = ml_dtypes.bfloat16
    f32 = np.float32
    asc = np.ascontiguousarray
    x = np.asarray(L0_attn_x, dtype=f32)
    wq, wk, wv, wo = (np.asarray(a, f32) for a in (L0_attn_wq, L0_attn_wk, L0_attn_wv, L0_attn_wo))
    bq, bk, bvv, bo = (np.asarray(a, f32) for a in (L0_attn_bq, L0_attn_bk, L0_attn_bv, L0_attn_bo))
    Ws = np.asarray(L0_ffn_Ws, f32)
    bs = np.asarray(L0_ffn_bs, f32)
    W1 = np.asarray(L0_ffn_W1, f32)
    b1 = np.asarray(L0_ffn_b1, f32)
    W2 = np.asarray(L0_ffn_W2, f32)
    b2 = np.asarray(L0_ffn_b2, f32)
    cc = np.asarray(L0_c, f32)
    wmod = np.asarray(L0_wmod, f32)

    NEG = np.float32(-1.0e30)
    r = np.arange(128)
    tri_l = np.where(r[:, None] >= r[None, :], np.float32(0), NEG).astype(f32)
    tri_u = np.where(r[:, None] <= r[None, :], np.float32(0), NEG).astype(f32)
    selm = np.zeros((128, 8), f32)
    for p in range(128):
        selm[p, (p % 16) // 2] = 1.0
    selmT = asc(selm.T)
    common = {
        "wqT": asc(wq.T.reshape(8, 128, E)),
        "wkT": asc(wk.T.reshape(8, 128, E)),
        "wvT": asc(wv.T.reshape(8, 128, E)),
        "woT": asc(wo.T.reshape(8, 128, E)),
        "bq": asc(bq.reshape(8, 128).T),
        "bk": asc(bk.reshape(8, 128).T),
        "bv_row": bvv.reshape(1, E),
        "bo_row": bo.reshape(1, E),
        "wmod": wmod,
        "Ws": Ws,
        "bs_row": bs.reshape(1, NEXP),
        "W1": W1,
        "b1c": asc(b1.reshape(NEXP, 32, 128).transpose(2, 0, 1).reshape(128, NEXP * 32)),
        "W2": W2,
        "b2r": b2,
        "id_bf": np.eye(128, dtype=bf),
        "id_f32": np.eye(128, dtype=f32),
        "tri_l": tri_l,
        "tri_u": tri_u,
        "selm": selm,
        "selmT": selmT,
        "ones_bf": np.ones((1, 128), dtype=bf),
    }
    in_maps = []
    for core in range(8):
        m = dict(common)
        m["x"] = asc(x[core])
        m["c_col"] = asc(cc[core].reshape(8, 128).T)
        in_maps.append(m)

    nc = _get_nc()
    res = run_bass_kernel_spmd(nc, in_maps, core_ids=list(range(8)))
    out = np.stack([np.asarray(res.results[c]["out"], dtype=f32) for c in range(8)], axis=0)
    return out


# version 06 (final)
# speedup vs baseline: 1.0443x; 1.0443x over previous
import numpy as np
import ml_dtypes
import concourse.bacc as bacc
import concourse.mybir as mybir
from concourse.bass_utils import run_bass_kernel_spmd
from contextlib import ExitStack

F32 = mybir.dt.float32
BF16 = mybir.dt.bfloat16
I16 = mybir.dt.int16
I32 = mybir.dt.int32
U32 = mybir.dt.uint32
U8 = mybir.dt.uint8
ALU = mybir.AluOpType
AF = mybir.ActivationFunctionType
AX = mybir.AxisListType

E = 1024
H = 16
WS = 512
NEXP = 8
FF = 4096
C = 640
CB = C // 16
KB = 1024


class Ctr:
    def __init__(self):
        self.v = 0

    def inc(self, n=1):
        self.v += n
        return self.v


def build_nc(NW=8, NITER=48, sim_single=False, debug=False):
    SEQ = NW * WS
    SB = SEQ // 16
    KGLOB = 8 * SEQ // NEXP
    nc = bacc.Bacc("TRN2", target_bir_lowering=False)

    dt = lambda n, s, d, k="ExternalInput", **kw: nc.dram_tensor(n, s, d, kind=k, **kw).ap()
    x_h = dt("x", [SEQ, E], F32)
    wT_h = {0: dt("wqT", [8, 128, E], F32), 1: dt("wkT", [8, 128, E], F32),
            2: dt("wvT", [8, 128, E], F32), 3: dt("woT", [8, 128, E], F32)}
    bq_h = dt("bq", [128, 8], F32)
    bk_h = dt("bk", [128, 8], F32)
    bv_row_h = dt("bv_row", [1, E], F32)
    bo_row_h = dt("bo_row", [1, E], F32)
    c_col_h = dt("c_col", [128, 8], F32)
    wmod_h = dt("wmod", [E, 6 * E], F32)
    Ws_h = dt("Ws", [E, NEXP], F32)
    bs_row_h = dt("bs_row", [1, NEXP], F32)
    W1_h = dt("W1", [NEXP, E, FF], F32)
    b1c_h = dt("b1c", [128, NEXP * 32], F32)
    W2_h = dt("W2", [NEXP, FF, E], F32)
    b2r_h = dt("b2r", [NEXP, E], F32)
    idbf_h = dt("id_bf", [128, 128], BF16)
    idf32_h = dt("id_f32", [128, 128], F32)
    tril_h = dt("tri_l", [128, 128], F32)
    triu_h = dt("tri_u", [128, 128], F32)
    sel_h = dt("selm", [128, 8], F32)
    selt_h = dt("selmT", [8, 128], F32)
    ones_bf_h = dt("ones_bf", [1, 128], BF16)

    out_h = dt("out", [SEQ, E], F32, k="ExternalOutput")

    h2m_hbm = dt("h2m_hbm", [SEQ, E], BF16, k="Internal")
    wfold_hbm = dt("wfold", [4, 8, 128, E], BF16, k="Internal")
    vT_hbm = dt("vT_hbm", [NEXP, SEQ], F32, k="Internal")
    rv_hbm = dt("rv_hbm", [NEXP, C], F32, k="Internal")
    mrows_hbm = dt("mrows_hbm", [4, E], BF16, k="Internal")
    cin_hbm = dt("cin", [8, SEQ], F32, k="Internal")
    if sim_single:
        cout_hbm = dt("cout", [64, SEQ], F32, k="Internal")
    else:
        cout_hbm = dt("cout", [64, SEQ], F32, k="Internal", addr_space="Shared")
    if debug:
        dbg_h1T = dt("dbg_h1T", [128, 8 * WS], BF16, k="Internal")
        dbg_q = dt("dbg_q", [128, 8 * WS], BF16, k="Internal")
        dbg_k = dt("dbg_k", [128, 8 * WS], BF16, k="Internal")
        dbg_v = dt("dbg_v", [128, 4 * 16 * 65], BF16, k="Internal")
        dbg_ao = dt("dbg_ao", [128, 4 * E], BF16, k="Internal")

    es = ExitStack()

    ARENA = 207 * KB
    arena_h = es.enter_context(nc.sbuf_tensor("arena", [128, ARENA], U8))
    arena_base = nc.lookup_mloc(arena_h).addr

    def alias(name, shape, dtype, off):
        h = nc.virtual_sbuf_tensor(name, shape, dtype)
        nc._tensor(h.name, h.shape, h.dtype, type="SB")
        ml = nc.lookup_mloc(h)
        ml.allocated = not nc.target_bir_lowering
        ml.addr = arena_base + off
        return h

    cur = [0]

    def nxt(nbytes, region=cur):
        o = region[0]
        region[0] = o + (nbytes + 63) // 64 * 64
        return o

    id_bf = alias("t_id_bf", [128, 128], BF16, nxt(256))
    id_f32 = alias("t_id_f32", [128, 128], F32, nxt(512))
    tri_l = alias("t_tri_l", [128, 128], F32, nxt(512))
    tri_u = alias("t_tri_u", [128, 128], F32, nxt(512))
    selm = alias("t_selm", [128, 8], F32, nxt(32))
    selmT = alias("t_selmT", [8, 128], F32, nxt(512))
    ones_bf = alias("t_ones_bf", [1, 128], BF16, nxt(256))
    c_col = alias("t_c_col", [128, 8], F32, nxt(32))
    silc = alias("t_silc", [128, 8], F32, nxt(32))
    modT = alias("t_modT", [128, 48], F32, nxt(192))
    ca1 = alias("t_ca1", [128, 8], F32, nxt(32))
    gaC = alias("t_gaC", [128, 8], F32, nxt(32))
    gbC = alias("t_gbC", [128, 8], F32, nxt(32))
    absga = alias("t_absga", [128, 8], F32, nxt(32))
    absgb = alias("t_absgb", [128, 8], F32, nxt(32))
    bq_col = alias("t_bq_col", [128, 8], F32, nxt(32))
    bk_col = alias("t_bk_col", [128, 8], F32, nxt(32))
    czero = alias("t_czero", [128, 1], F32, nxt(4))
    cone = alias("t_cone", [128, 1], F32, nxt(4))
    chalf = alias("t_chalf", [128, 1], F32, nxt(4))
    cquart = alias("t_cquart", [128, 1], F32, nxt(4))
    ctwo = alias("t_ctwo", [128, 1], F32, nxt(4))
    crecE = alias("t_crecE", [128, 1], F32, nxt(4))
    ceps = alias("t_ceps", [128, 1], F32, nxt(4))
    ck = alias("t_ck", [128, 1], F32, nxt(4))
    cone8 = alias("t_cone8", [8, 1], F32, nxt(4))
    chalf8 = alias("t_chalf8", [8, 1], F32, nxt(4))
    lo_t = alias("t_lo", [128, 1], F32, nxt(4))
    hi_t = alias("t_hi", [128, 1], F32, nxt(4))
    tm_t = alias("t_tm", [128, 1], F32, nxt(4))
    ge_t = alias("t_ge", [128, 1], F32, nxt(4))
    e1_t = alias("t_e1", [128, 1], F32, nxt(4))
    e2_t = alias("t_e2", [128, 1], F32, nxt(4))
    cnt_t = alias("t_cnt", [128, 1], F32, nxt(4))
    cntE = alias("t_cntE", [8, 1], F32, nxt(4))
    thr8 = alias("t_thr8", [8, 1], F32, nxt(4))
    mrow_bf = alias("t_mrow_bf", [8, 128], BF16, nxt(256))
    rows_sb = [alias("t_rows%d" % i, [1, E], BF16, nxt(E * 2)) for i in range(4)]
    bs_bf = alias("t_bs_bf", [1, NEXP], BF16, nxt(64))
    bv_sb = alias("t_bvsb", [1, E], BF16, nxt(E * 2))
    bo_sb = alias("t_bosb", [1, E], BF16, nxt(E * 2))
    Wsb = alias("t_Wsb", [128, 64], BF16, nxt(128))
    b1c = alias("t_b1c", [128, NEXP * 32], F32, nxt(NEXP * 32 * 4))
    b2row_bf = alias("t_b2row", [1, E], BF16, nxt(2 * KB))
    mean_t = alias("t_mean", [128, 1], F32, nxt(4))
    var_t = alias("t_var", [128, 1], F32, nxt(4))
    rstd_t = alias("t_rstd", [128, 1], F32, nxt(4))
    nmr_t = alias("t_nmr", [128, 1], F32, nxt(4))
    sq_t = alias("t_sq", [128, E], BF16, nxt(2 * KB))
    c16th = alias("t_c16th", [8, 1], F32, nxt(4))
    rz_t = alias("t_rz", [128, 1], F32, nxt(4))
    kpad = alias("t_kpad", [128, 128], BF16, nxt(256))
    vpad = alias("t_vpad", [128, 16 * 65], BF16, nxt(16 * 65 * 2))
    gaTb = alias("t_gaTb", [128, E], BF16, nxt(2 * KB))
    gbTb = alias("t_gbTb", [128, E], BF16, nxt(2 * KB))
    cbTb = alias("t_cbTb", [128, E], BF16, nxt(2 * KB))
    sbTb = alias("t_sbTb", [128, E], BF16, nxt(2 * KB))
    bvTb = alias("t_bvTb", [128, E], BF16, nxt(2 * KB))
    boTb = alias("t_boTb", [128, E], BF16, nxt(2 * KB))
    
    idx_all = alias("t_idx_all", [128, NEXP * CB], I16, nxt(NEXP * CB * 2))
    rvsm_all = alias("t_rvsm", [128, NEXP * (C // 128)], F32, nxt(NEXP * (C // 128) * 4))
    probe_tok = alias("t_probe_tok", [128, 8], F32, nxt(32))
    v16 = alias("t_v16", [16, SB], F32, nxt(SB * 4))
    cmp16 = alias("t_cmp16", [16, SB], F32, nxt(SB * 4))
    clean16 = alias("t_clean16", [16, SB], F32, nxt(SB * 4))
    m16 = alias("t_m16", [16, SB], I32, nxt(SB * 4))
    zz16 = alias("t_zz16", [16, SB], F32, nxt(SB * 4))
    ids16 = alias("t_ids16", [16, SB], F32, nxt(SB * 4))
    idxlo = alias("t_idxlo", [16, CB], I16, nxt(CB * 2))
    idflo = alias("t_idflo", [16, CB], F32, nxt(CB * 4))
    rv16 = alias("t_rv16", [16, CB], F32, nxt(CB * 4))
    nf_t = alias("t_nf", [1, 1], U32, nxt(4))
    nff_t = alias("t_nff", [1, 1], F32, nxt(4))
    nfb_t = alias("t_nfb", [16, 1], F32, nxt(4))
    b2Tsb = alias("t_b2Tsb", [128, E], BF16, nxt(2 * KB))
    pstage = alias("t_pstage", [8, 128], F32, nxt(512))
    wsstage = alias("t_wsstage", [128, 64], F32, nxt(256))

    shared_end = cur[0]
    REG0 = (shared_end + 63) // 64 * 64
    assert REG0 <= 49 * KB, REG0

    acur = [REG0]
    xw = alias("t_xw", [128, 2 * 4 * E], F32, nxt(32 * KB, acur))
    wslot = alias("t_wslot", [128, 2 * 8 * E], BF16, nxt(32 * KB, acur))
    kroll_off = nxt(24 * KB, acur)
    kroll = alias("t_kroll", [128, 3 * 8 * WS], BF16, kroll_off)
    foldtile = alias("t_foldtile", [128, E], BF16, kroll_off)
    vroll_off = nxt(3 * 4 * 16 * 65 * 2, acur)
    vroll = alias("t_vroll", [128, 3 * 4 * 16 * 65], BF16, vroll_off)
    wstage = alias("t_wstage", [128, E], F32, vroll_off)
    rowstage = alias("t_rowstage", [1, E], F32, vroll_off + 4 * KB)
    qbuf = alias("t_qbuf", [128, 8 * WS], BF16, nxt(8 * KB, acur))
    h1T = alias("t_h1T", [128, 8 * WS], BF16, nxt(8 * KB, acur))
    attnp = alias("t_attnp", [128, 2 * WS], BF16, nxt(2 * KB, acur))
    ao_off = nxt(8 * KB, acur)
    attnout = alias("t_attnout", [128, 4 * E], BF16, ao_off)
    h1t = alias("t_h1t", [128, 2 * E], BF16, ao_off)
    aoT = alias("t_aoT", [128, 8 * 128], BF16, nxt(2 * KB, acur))
    h2mt = alias("t_h2mt", [128, 2 * E], BF16, nxt(4 * KB, acur))
    assert acur[0] <= ARENA, acur[0]

    mcur = [REG0]
    outacc = alias("t_outacc", [128, 5 * E], F32, nxt(20 * KB, mcur))
    xin_off = nxt(2 * 8 * C * 2, mcur)
    xinT = alias("t_xinT", [128, 2 * 8 * C], BF16, xin_off)
    bisdump = alias("t_bisdump", [128, SEQ // 2], BF16, xin_off)
    search = alias("t_search", [128, SEQ // 2], F32, nxt(SEQ // 2 * 4, mcur))
    W1_off = nxt(32 * KB, mcur)
    W1blk = alias("t_W1blk", [128, 2 * 8 * KB], BF16, W1_off)
    probs_sb = alias("t_probs_sb", [8, SEQ], F32, W1_off)
    W2blk = alias("t_W2blk", [128, 2 * 8 * KB], BF16, nxt(32 * KB, mcur))
    vTt = alias("t_vTt", [8, SEQ], F32, nxt(SEQ * 4, mcur))
    selT_off = nxt(max(SEQ * 4, 2 * 8 * C * 2), mcur)
    selT = alias("t_selT", [8, SEQ], F32, selT_off)
    hexp = alias("t_hexp", [128, 2 * 8 * C], BF16, selT_off)
    assert mcur[0] <= ARENA, mcur[0]

    pb = [es.enter_context(nc.psum_tensor("pb%d" % i, [128, 512], F32)) for i in range(7)]
    pbf = es.enter_context(nc.psum_tensor("pbf", [128, 1024], BF16))

    sem = lambda n: es.enter_context(nc.semaphore(n))
    sT = sem("sT"); cT = Ctr()
    sV = sem("sV"); cV = Ctr()
    sS = sem("sS"); cS = Ctr()
    sG = sem("sG"); cG = Ctr()
    qS = sem("qS"); cQS = Ctr()
    qA = sem("qA"); cQA = Ctr()
    csem = sem("csem")
    gsems = [sem("gs%d" % e) for e in range(NEXP)]
    ssems = [sem("ss%d" % e) for e in range(NEXP)]
    gq0 = sem("gq0"); cGQ0 = Ctr()
    qX = sem("qX"); cQX = Ctr()
    qW = [sem("qW%d" % i) for i in range(4)]
    cQW = [Ctr() for _ in range(4)]
    qC = sem("qC"); cQC = Ctr()
    qF = sem("qF"); cQF = Ctr()
    qO = sem("qO"); cQO = Ctr()
    qH = sem("qH"); cQH = Ctr()
    qP = sem("qP"); cQP = Ctr()
    qVT = sem("qVT"); cQVT = Ctr()
    gq2 = sem("gq2"); cGQ2 = Ctr()
    gq3 = sem("gq3"); cGQ3 = Ctr()

    block = es.enter_context(nc.Block())
    T, Vv, Sc, Sy, Gp = block.tensor, block.vector, block.scalar, block.sync, block.gpsimd

    bank_v = {i: 0 for i in range(8)}
    bank_s = {i: 0 for i in range(8)}

    def bank_wait(tensor, i):
        if bank_v[i]:
            tensor.wait_ge(sV, bank_v[i])
        if bank_s[i]:
            tensor.wait_ge(sS, bank_s[i])

    def pe_bump(tensor):
        tensor.matmul(pb[6][0:1, 0:1], id_f32[0:1, 0:1], id_f32[0:1, 0:1],
                      start=True, stop=True).then_inc(sT, 1)
        cT.inc()

    def p0_loads(sync):
        for h_, t_ in [(idbf_h, id_bf), (idf32_h, id_f32), (tril_h, tri_l),
                       (triu_h, tri_u), (sel_h, selm), (selt_h, selmT),
                       (ones_bf_h, ones_bf), (c_col_h, c_col),
                       (bq_h, bq_col), (bk_h, bk_col), (b1c_h, b1c)]:
            sync.dma_start(out=t_[:, :], in_=h_[:, :]).then_inc(qS, 16)
            cQS.inc(16)
    Sy(p0_loads)
    p0_loads_done = cQS.v

    def p0_gload1(sync):
        sync.dma_start(out=wsstage[:, :].rearrange("p (c e) -> p c e", c=8),
                       in_=Ws_h[:, :].rearrange("(c p) e -> p c e", p=128)).then_inc(qF, 16)
        cQF.inc(16)
    Sy(p0_gload1)
    ws_ld = cQF.v

    def p0_gcast1(vector):
        vector.wait_ge(qF, ws_ld)
        vector.tensor_copy(Wsb[:, :], wsstage[:, :]).then_inc(sV, 1)
        cV.inc()
    Vv(p0_gcast1)
    wsb_done = cV.v

    rows_cast = []
    for (rh, rt, n) in [(bs_row_h, bs_bf, NEXP), (bv_row_h, bv_sb, E), (bo_row_h, bo_sb, E)]:
        def rld(sync, rh=rh, n=n, gate=(rows_cast[-1] if rows_cast else wsb_done)):
            sync.wait_ge(sV, gate)
            sync.dma_start(out=rowstage[:, 0:n], in_=rh[:, :]).then_inc(qF, 16)
            cQF.inc(16)
        Sy(rld)
        rlv = cQF.v

        def rcast(vector, rt=rt, n=n, rlv=rlv):
            vector.wait_ge(qF, rlv)
            vector.tensor_copy(rt[:, 0:n], rowstage[:, 0:n]).then_inc(sV, 1)
            cV.inc()
        Vv(rcast)
        rows_cast.append(cV.v)
    p0_gloads_done = rows_cast[-1]

    xload_marks = {}

    def p0_xload(scalar):
        scalar.dma_start(out=xw[:, 0:4 * E].rearrange("p (t f) -> p t f", t=4),
                         in_=x_h[0:WS, :].rearrange("(t p) f -> p t f", p=128)).then_inc(qX, 16)
        cQX.inc(16)
    Sc(p0_xload)
    xload_marks[0] = cQX.v
    rows_done = p0_gloads_done

    def p0_consts(vector):
        vector.memset(czero[:, :], 0.0)
        vector.memset(cone[:, :], 1.0)
        vector.memset(chalf[:, :], 0.5)
        vector.memset(cquart[:, :], 0.25)
        vector.memset(ctwo[:, :], 2.0)
        vector.memset(crecE[:, :], 1.0 / E)
        vector.memset(ceps[:, :], 1e-5)
        vector.memset(ck[:, :], float(KGLOB))
        vector.memset(cone8[:, :], 1.0)
        vector.memset(chalf8[:, :], 0.5)
        vector.memset(c16th[:, :], 1.0 / 16.0)
        vector.memset(lo_t[:, :], 0.0)
        vector.memset(hi_t[:, :], 1.0)
        vector.memset(tm_t[:, :], 0.5)
        vector.memset(kpad[:, :], -1.0)
        vector.memset(vpad[:, :], -1.0)
        vector.memset(zz16[:, :], 0.0)
        vector.memset(idx_all[:, :], 0)
        vector.memset(nfb_t[:, :], 0.0)
        vector.memset(nff_t[:, :], 0.0)
        vector.memset(cmp16[:, :], -1.0)
        vector.drain()
        vector.memset(vpad[:, :].rearrange("p (h x) -> p h x", x=65)[:, :, 64:65], 1.0)
        vector.drain()
        vector.tensor_copy(rz_t[:, :], czero[:, :]).then_inc(sV, 1)
        cV.inc()
    Vv(p0_consts)
    consts_done = cV.v

    def p0_silu(scalar):
        scalar.wait_ge(qS, p0_loads_done)
        scalar.activation(out=silc[:, :], in_=c_col[:, :], func=AF.Silu).then_inc(sS, 1)
        cS.inc()
    Sc(p0_silu)
    silu_done = cS.v

    modmm_last = [0]
    for j in range(48):
        def ld(sync, j=j):
            if modmm_last[0]:
                sync.wait_ge(sT, modmm_last[0])
            if j == 0:
                sync.wait_ge(qS, p0_loads_done)
            sync.dma_start(out=wstage[:, :].rearrange("p (d f) -> p d f", d=8),
                           in_=wmod_h[:, j * 128:(j + 1) * 128].rearrange("(d p) f -> p d f", p=128)
                           ).then_inc(qS, 16)
            cQS.inc(16)
        Sy(ld)
        ldv = cQS.v

        def mm(tensor, j=j, ldv=ldv):
            tensor.wait_ge(qS, ldv)
            if j == 0:
                tensor.wait_ge(sS, silu_done)
            bank_wait(tensor, 0)
            for d in range(8):
                tensor.matmul(pb[0][:, (j % 4) * 8 + d:(j % 4) * 8 + d + 1],
                              wstage[:, d * 128:(d + 1) * 128], silc[:, d:d + 1],
                              start=True, stop=True)
            pe_bump(tensor)
        T(mm)
        modmm_last[0] = cT.v

        def rd(vector, j=j, mmv=cT.v):
            vector.wait_ge(sT, mmv)
            vector.tensor_reduce(modT[:, j:j + 1],
                                 pb[0][:, (j % 4) * 8:(j % 4) * 8 + 8],
                                 axis=AX.X, op=ALU.add).then_inc(sV, 1)
            cV.inc()
        Vv(rd)
        if j % 4 == 3:
            bank_v[0] = cV.v
    mod_mm_done = modmm_last[0]
    mod_done = cV.v

    def p0_clamp1(vector):
        vector.wait_ge(sV, mod_done)
        vector.tensor_scalar(ca1[:, :], modT[:, 8:16], cone[:, :], None, ALU.add)
        vector.tensor_scalar(gaC[:, :], modT[:, 16:24], czero[:, :], None, ALU.is_ge)
        vector.tensor_scalar(gbC[:, :], modT[:, 40:48], czero[:, :], None, ALU.is_ge)
        vector.drain()
        vector.tensor_scalar(gaC[:, :], gaC[:, :], ctwo[:, :], cone[:, :], ALU.mult, ALU.subtract)
        vector.tensor_scalar(gbC[:, :], gbC[:, :], ctwo[:, :], cone[:, :], ALU.mult, ALU.subtract)
        vector.drain()
        vector.tensor_copy(e1_t[:, :], czero[:, :]).then_inc(sV, 1)
        cV.inc()
    Vv(p0_clamp1)
    clamp1_done = cV.v

    def p0_abs(scalar):
        scalar.wait_ge(sV, clamp1_done)
        scalar.activation(out=absga[:, :], in_=modT[:, 16:24], func=AF.Abs)
        scalar.activation(out=absgb[:, :], in_=modT[:, 40:48], func=AF.Abs).then_inc(sS, 1)
        cS.inc()
    Sc(p0_abs)
    abs_done = cS.v

    def p0_clamp2(vector):
        vector.wait_ge(sS, abs_done)
        vector.tensor_scalar(absga[:, :], absga[:, :], cquart[:, :], None, ALU.max)
        vector.tensor_scalar(absgb[:, :], absgb[:, :], cquart[:, :], None, ALU.max)
        vector.drain()
        vector.tensor_tensor(gaC[:, :], gaC[:, :], absga[:, :], ALU.mult)
        vector.tensor_tensor(gbC[:, :], gbC[:, :], absgb[:, :], ALU.mult)
        vector.drain()
        vector.tensor_copy(e1_t[:, :], czero[:, :]).then_inc(sV, 1)
        cV.inc()
    Vv(p0_clamp2)
    clamp_done = cV.v

    bc_jobs = [(0, gaC, None, gaTb), (1, gbC, None, gbTb),
               (2, modT, slice(32, 40), cbTb), (3, modT, slice(24, 32), sbTb)]
    mrow_fr = [0]
    for (ri, src, sl, dstb) in bc_jobs:
        def tp(tensor, src=src, sl=sl):
            tensor.wait_ge(sV, clamp_done)
            bank_wait(tensor, 2)
            s_ap = src[:, sl] if sl is not None else src[:, :]
            tensor.transpose(pb[2][0:8, 0:128], s_ap, id_f32[:, :])
            pe_bump(tensor)
        T(tp)
        tpv = cT.v

        def ev(vector, tpv=tpv, prev_fr=mrow_fr[0]):
            vector.wait_ge(sT, tpv)
            if prev_fr:
                vector.wait_ge(qS, prev_fr)
            vector.tensor_copy(mrow_bf[:, :], pb[2][0:8, 0:128]).then_inc(sV, 1)
            cV.inc()
        Vv(ev)
        bank_v[2] = cV.v
        evv = cV.v

        def fold_row(sync, ri=ri, evv=evv):
            sync.wait_ge(sV, evv)
            sync.dma_start(out=mrows_hbm[ri:ri + 1, :].rearrange("r (d f) -> (r d) f", d=8),
                           in_=mrow_bf[:, :]).then_inc(qS, 16)
            cQS.inc(16)
            sync.wait_ge(qS, cQS.v)
            sync.dma_start(out=rows_sb[ri][:, :], in_=mrows_hbm[ri:ri + 1, :]).then_inc(qS, 16)
            cQS.inc(16)
        Sy(fold_row)
        frv = cQS.v
        mrow_fr[0] = frv

        def bc(gpsimd, ri=ri, frv=frv, dstb=dstb):
            gpsimd.wait_ge(qS, frv)
            gpsimd.partition_broadcast(dstb[:, :], rows_sb[ri][:, :], channels=128)
            gpsimd.drain()
            gpsimd.memset(nfb_t[0:1, 0:1], 0.0).then_inc(sG, 1)
            cG.inc()
        Gp(bc)
    for (rt, dstb) in [(bv_sb, bvTb), (bo_sb, boTb)]:
        def bc2(gpsimd, rt=rt, dstb=dstb):
            gpsimd.wait_ge(sV, rows_done)
            gpsimd.partition_broadcast(dstb[:, :], rt[:, :], channels=128)
            gpsimd.drain()
            gpsimd.memset(nfb_t[0:1, 0:1], 0.0).then_inc(sG, 1)
            cG.inc()
        Gp(bc2)
    bcast_g_done = cG.v

    def bcfin(vector):
        vector.wait_ge(sG, bcast_g_done)
        vector.tensor_copy(rz_t[:, :], czero[:, :]).then_inc(sV, 1)
        cV.inc()
    Vv(bcfin)
    bcast_done = cV.v

    fold_free_v = [0]
    fold_free_t = [0]
    for (mi, do_fold, bcol) in [(0, True, bq_col), (1, True, bk_col),
                                (2, True, None), (3, False, None)]:
        mv_last = 0
        for d in range(8):
            def ld(scalar, mi=mi, d=d):
                if fold_free_v[0]:
                    scalar.wait_ge(sV, fold_free_v[0])
                if fold_free_t[0]:
                    scalar.wait_ge(sT, fold_free_t[0])
                if mi == 0 and d == 0:
                    scalar.wait_ge(sT, mod_mm_done)
                scalar.dma_start(out=wstage[:, :], in_=wT_h[mi][d, :, :]).then_inc(qA, 16)
                cQA.inc(16)
            Sc(ld)
            ldv = cQA.v

            if bcol is not None:
                def mv(tensor, d=d, ldv=ldv):
                    tensor.wait_ge(qA, ldv)
                    tensor.wait_ge(sV, clamp_done)
                    bank_wait(tensor, 1)
                    for fc in range(8):
                        tensor.matmul(pb[1][:, fc * 8 + d:fc * 8 + d + 1],
                                      wstage[:, fc * 128:(fc + 1) * 128],
                                      modT[:, d:d + 1], start=True, stop=True)
                    pe_bump(tensor)
                T(mv)
                mv_last = cT.v
                fold_free_t[0] = cT.v

            def fold(vector, mi=mi, d=d, ldv=ldv, do_fold=do_fold):
                vector.wait_ge(qA, ldv)
                if do_fold:
                    vector.wait_ge(sV, clamp_done)
                    vector.tensor_scalar(foldtile[:, :], wstage[:, :], ca1[:, d:d + 1],
                                         None, ALU.mult).then_inc(sV, 1)
                else:
                    vector.tensor_copy(foldtile[:, :], wstage[:, :]).then_inc(sV, 1)
                cV.inc()
            Vv(fold)
            fv = cV.v

            def st(scalar, mi=mi, d=d, fv=fv):
                scalar.wait_ge(sV, fv)
                scalar.dma_start(out=wfold_hbm[mi, d, :, :], in_=foldtile[:, :]).then_inc(qA, 16)
                cQA.inc(16)
                fold_free_v[0] = fv
            Sc(st)
        if bcol is not None:
            def bias(vector, bcol=bcol, mv_last=mv_last):
                vector.wait_ge(sT, mv_last)
                vector.wait_ge(qS, p0_loads_done)
                vector.tensor_reduce(probe_tok[:, 0:8],
                                     pb[1][:, 0:64].rearrange("p (fc d) -> p fc d", d=8),
                                     axis=AX.X, op=ALU.add)
                vector.drain()
                vector.tensor_tensor(bcol[:, :], bcol[:, :], probe_tok[:, 0:8], ALU.add)
                vector.drain()
                vector.tensor_copy(rz_t[:, :], czero[:, :]).then_inc(sV, 1)
                cV.inc()
            Vv(bias)
            bank_v[1] = cV.v
    fold_done_qA = cQA.v

    def p0_vones(vector):
        vector.wait_ge(qA, fold_done_qA)
        vector.memset(vroll[:, :].rearrange("p (c x) -> p c x", x=65)[:, :, 64:65], 1.0)
        vector.drain()
        vector.tensor_copy(rz_t[:, :], czero[:, :]).then_inc(sV, 1)
        cV.inc()
    Vv(p0_vones)
    vones_done = cV.v

    WQ, WK, WV, WO = 0, 1, 2, 3
    wslot_ap = lambda s: wslot[:, s * 8 * E:(s + 1) * 8 * E]
    wload_marks = {}
    wslot_last = {}

    def load_w(sync, w, m, s, wait_t):
        if wait_t:
            sync.wait_ge(sT, wait_t)
        prior = wslot_last.get(s)
        if prior is not None:
            sync.wait_ge(qW[prior[0]], prior[1])
        sync.wait_ge(qA, fold_done_qA)
        sync.dma_start(out=wslot_ap(s).rearrange("p (d f) -> p d f", d=8),
                       in_=wfold_hbm[m, :, :, :].rearrange("d p f -> p d f")).then_inc(qW[m], 16)
        cQW[m].inc(16)
        wload_marks[(w, m)] = cQW[m].v
        wslot_last[s] = (m, cQW[m].v)

    def p1_init_loads(sync):
        load_w(sync, 0, WK, 0, None)
        load_w(sync, 0, WV, 1, None)
    Sy(p1_init_loads)

    h1_ev = {}
    qk_ev = {}
    v_ev = {}
    q_ev = {}
    x1_written = {}
    h2m_written = {}
    xslot_guard = {}
    attn_last_av = {0: 0, 1: 0}
    probsT_parts = []

    def emit_h1(w):
        for tt in range(4):
            xt = xw[:, ((w % 2) * 4 + tt) * E:((w % 2) * 4 + tt + 1) * E]

            def stats(vector, xt=xt, w=w, tt=tt):
                vector.wait_ge(qX, xload_marks[w])
                if w == 0 and tt == 0:
                    vector.wait_ge(sV, consts_done)
                vector.tensor_reduce(mean_t[:, :], xt, axis=AX.X, op=ALU.add, negate=True)
                vector.tensor_tensor(sq_t[:, :], xt, xt, ALU.mult)
                vector.drain()
                vector.tensor_reduce(var_t[:, :], sq_t[:, :], axis=AX.X, op=ALU.add)
                vector.drain()
                vector.tensor_scalar(mean_t[:, :], mean_t[:, :], crecE[:, :], None, ALU.mult)
                vector.tensor_scalar(var_t[:, :], var_t[:, :], crecE[:, :], None, ALU.mult)
                vector.drain()
                vector.tensor_tensor(sq_t[:, 0:1], mean_t[:, :], mean_t[:, :], ALU.mult)
                vector.drain()
                vector.tensor_tensor(var_t[:, :], var_t[:, :], sq_t[:, 0:1], ALU.subtract).then_inc(sV, 1)
                cV.inc()
            Vv(stats)
            stv = cV.v

            def rstd(scalar, stv=stv):
                scalar.wait_ge(sV, stv)
                scalar.activation(out=rstd_t[:, :], in_=var_t[:, :], func=AF.Sqrt,
                                  bias=ceps[:, :]).then_inc(sS, 1)
                cS.inc()
            Sc(rstd)
            rsv = cS.v

            def norm(vector, xt=xt, tt=tt, rsv=rsv):
                vector.wait_ge(sS, rsv)
                g = h1t_guard.get(tt % 2)
                if g:
                    vector.wait_ge(sT, g)
                gg = h1tg_g.get(tt % 2)
                if gg:
                    vector.wait_ge(sG, gg)
                vector.reciprocal(rstd_t[:, :], rstd_t[:, :])
                vector.drain()
                vector.tensor_scalar(h1t[:, (tt % 2) * E:(tt % 2 + 1) * E], xt,
                                     mean_t[:, :], rstd_t[:, :],
                                     ALU.add, ALU.mult).then_inc(sV, 1)
                cV.inc()
            Vv(norm)
            nv = cV.v

            def tp(tensor, tt=tt, nv=nv):
                tensor.wait_ge(sV, nv)
                bank_wait(tensor, 7)
                for fc in range(8):
                    mmi = tensor.transpose(pbf[:, fc * 128:(fc + 1) * 128],
                                     h1t[:, (tt % 2) * E + fc * 128:(tt % 2) * E + (fc + 1) * 128],
                                     id_bf[:, :])
                mmi.then_inc(sT, 1)
                cT.inc()
            T(tp)
            h1t_guard[tt % 2] = cT.v
            tpv = cT.v

            def tev(vector, tt=tt, tpv=tpv, w=w):
                vector.wait_ge(sT, tpv)
                if debug and w == 1 and tt == 0:
                    vector.wait_ge(qP, 64)
                vector.tensor_copy(
                    h1T[:, :].rearrange("p (d t) -> p d t", d=8)[:, :, tt * 128:(tt + 1) * 128],
                    pbf[:, :].rearrange("p (d t) -> p d t", d=8)).then_inc(sV, 1)
                cV.inc()
            Vv(tev)
            bank_v[7] = cV.v
        h1_ev[w] = cV.v

    h1t_guard = {}
    h1tg_g = {}

    def emit_proj_K(w):
        for fc in range(8):
            bank = fc % 2

            def mm(tensor, fc=fc, w=w, bank=bank):
                if fc == 0:
                    tensor.wait_ge(qW[WK], wload_marks[(w, WK)])
                    tensor.wait_ge(sV, h1_ev[w])
                bank_wait(tensor, bank)
                for d in range(8):
                    mmi = tensor.matmul(pb[bank][:, :],
                                  wslot_ap(0)[:, d * E + fc * 128: d * E + (fc + 1) * 128],
                                  h1T[:, d * WS:(d + 1) * WS],
                                  start=(d == 0), stop=(d == 7))
                mmi.then_inc(sT, 1)
                cT.inc()
            T(mm)
            mmv = cT.v

            def ev(vector, fc=fc, w=w, mmv=mmv, bank=bank):
                vector.wait_ge(sT, mmv)
                vector.tensor_scalar(
                    kroll[:, ((w % 3) * 8 + fc) * WS:((w % 3) * 8 + fc + 1) * WS],
                    pb[bank][:, :], bk_col[:, fc:fc + 1], None, ALU.add).then_inc(sV, 1)
                cV.inc()
            Vv(ev)
            bank_v[bank] = cV.v
        qk_ev[w] = cV.v

    def emit_proj_V(w):
        for tt in range(4):
            for half in range(2):
                bank = (tt * 2 + half) % 2

                def mm(tensor, tt=tt, half=half, w=w, bank=bank):
                    if tt == 0 and half == 0:
                        tensor.wait_ge(qW[WV], wload_marks[(w, WV)])
                    bank_wait(tensor, bank)
                    for d in range(8):
                        mmi = tensor.matmul(pb[bank][:, :],
                                      h1T[:, d * WS + tt * 128: d * WS + (tt + 1) * 128],
                                      wslot_ap(1)[:, d * E + half * 512: d * E + (half + 1) * 512],
                                      start=(d == 0), stop=(d == 7))
                    mmi.then_inc(sT, 1)
                    cT.inc()
                T(mm)
                mmv = cT.v

                def ev(vector, tt=tt, half=half, w=w, mmv=mmv, bank=bank):
                    vector.wait_ge(sT, mmv)
                    if tt == 0 and half == 0:
                        vector.wait_ge(sV, max(bcast_done, vones_done))
                    dst = vroll[:, ((w % 3) * 4 + tt) * 16 * 65:((w % 3) * 4 + tt + 1) * 16 * 65]
                    dst = dst.rearrange("p (h x) -> p h x", x=65)[:, half * 8:(half + 1) * 8, 0:64]
                    vector.tensor_tensor(
                        dst, pb[bank][:, :].rearrange("p (h f) -> p h f", f=64),
                        bvTb[:, half * 512:(half + 1) * 512].rearrange("p (h f) -> p h f", f=64),
                        ALU.add).then_inc(sV, 1)
                    cV.inc()
                Vv(ev)
                bank_v[bank] = cV.v
        v_ev[w] = cV.v

    def emit_proj_Q(w):
        for fc in range(8):
            bank = fc % 2

            def mm(tensor, fc=fc, w=w, bank=bank):
                if fc == 0:
                    tensor.wait_ge(qW[WQ], wload_marks[(w, WQ)])
                    if debug and w == 1:
                        tensor.wait_ge(qP, 64)
                bank_wait(tensor, bank)
                for d in range(8):
                    mmi = tensor.matmul(pb[bank][:, :],
                                  wslot_ap(0)[:, d * E + fc * 128: d * E + (fc + 1) * 128],
                                  h1T[:, d * WS:(d + 1) * WS],
                                  start=(d == 0), stop=(d == 7))
                mmi.then_inc(sT, 1)
                cT.inc()
            T(mm)
            mmv = cT.v

            def ev(vector, fc=fc, mmv=mmv, bank=bank):
                vector.wait_ge(sT, mmv)
                vector.tensor_scalar(qbuf[:, fc * WS:(fc + 1) * WS],
                                     pb[bank][:, :], bq_col[:, fc:fc + 1],
                                     None, ALU.add).then_inc(sV, 1)
                cV.inc()
            Vv(ev)
            bank_v[bank] = cV.v
        q_ev[w] = cV.v

    def emit_attend(wa):
        lastc = {}
        for d_ in range(4):
            lc = 0
            for c_ in range(12):
                win_ = wa - 1 + c_ // 4
                pad_ = (win_ < 0) or (win_ >= NW)
                if pad_ or (0 <= c_ - d_ <= 8):
                    lc = c_
            lastc[d_] = lc
        for h in range(H):
            fcH, hh = h // 2, h % 2
            av_started = [False] * 4

            def headwaits(tensor, wa=wa, h=h):
                if h == 0:
                    kw = wa + 1 if wa + 1 < NW else wa
                    tensor.wait_ge(sV, qk_ev[kw])
                    tensor.wait_ge(sV, v_ev[kw])
                    tensor.wait_ge(sV, q_ev[wa])
            T(headwaits)

            for c in range(12):
                win = wa - 1 + c // 4
                pad = (win < 0) or (win >= NW)
                if pad:
                    dlo, dhi = 0, 3
                else:
                    dlo, dhi = max(0, c - 8), min(3, c)
                    if dlo > dhi:
                        continue

                def smm(tensor, wa=wa, h=h, c=c, pad=pad, win=win, fcH=fcH, hh=hh):
                    bank_wait(tensor, c % 2)
                    if pad:
                        lhs = kpad[hh * 64:hh * 64 + 64, :]
                    else:
                        slot = win % 3
                        lhs = kroll[hh * 64:hh * 64 + 64,
                                    (slot * 8 + fcH) * WS + (c % 4) * 128:
                                    (slot * 8 + fcH) * WS + (c % 4 + 1) * 128]
                    tensor.matmul(pb[c % 2][:, :], lhs,
                                  qbuf[hh * 64:hh * 64 + 64, fcH * WS:(fcH + 1) * WS],
                                  start=True, stop=True).then_inc(sT, 1)
                    cT.inc()
                T(smm)
                smv = cT.v

                tri_need = []
                if not pad:
                    if dlo <= c <= dhi:
                        tri_need.append((c, tri_l))
                    if dlo <= c - 8 <= dhi:
                        tri_need.append((c - 8, tri_u))
                mv = None
                if tri_need:
                    def tri(vector, c=c, tri_need=tri_need, smv=smv):
                        vector.wait_ge(sT, smv)
                        for (d, tmask) in tri_need:
                            vector.tensor_tensor(pb[c % 2][:, d * 128:(d + 1) * 128],
                                                 pb[c % 2][:, d * 128:(d + 1) * 128],
                                                 tmask[:, :], ALU.add)
                        vector.drain()
                        vector.tensor_copy(e2_t[:, :], czero[:, :]).then_inc(sV, 1)
                        cV.inc()
                    Vv(tri)
                    mv = cV.v

                def expc(scalar, c=c, dlo=dlo, dhi=dhi, smv=smv, mv=mv):
                    if mv is not None:
                        scalar.wait_ge(sV, mv)
                    else:
                        scalar.wait_ge(sT, smv)
                    if attn_last_av[c % 2]:
                        scalar.wait_ge(sT, attn_last_av[c % 2])
                    scalar.activation(
                        out=attnp[:, (c % 2) * WS + dlo * 128:(c % 2) * WS + (dhi + 1) * 128],
                        in_=pb[c % 2][:, dlo * 128:(dhi + 1) * 128],
                        func=AF.Exp).then_inc(sS, 1)
                    cS.inc()
                Sc(expc)
                expv = cS.v
                bank_s[c % 2] = expv

                def avc(tensor, h=h, c=c, dlo=dlo, dhi=dhi, expv=expv, pad=pad, win=win,
                        av_started=av_started):
                    tensor.wait_ge(sS, expv)
                    for d in range(dlo, dhi + 1):
                        if not av_started[d]:
                            bank_wait(tensor, 2 + d)
                        if pad:
                            rhs = vpad[:, h * 65:(h + 1) * 65]
                        else:
                            slot = win % 3
                            base = (slot * 4 + c % 4) * 16 * 65
                            rhs = vroll[:, base + h * 65: base + (h + 1) * 65]
                        mmi = tensor.matmul(pb[2 + d][:, 0:65],
                                      attnp[:, (c % 2) * WS + d * 128:(c % 2) * WS + (d + 1) * 128],
                                      rhs, start=(not av_started[d]), stop=(c == lastc[d]))
                        av_started[d] = True
                    mmi.then_inc(sT, 1)
                    cT.inc()
                T(avc)
                attn_last_av[c % 2] = cT.v
            av_head_done = cT.v

            def avev(vector, h=h, av_head_done=av_head_done, wa=wa):
                vector.wait_ge(sT, av_head_done)
                if debug and wa == 1 and h == 0:
                    vector.wait_ge(qP, 80)
                for d in range(4):
                    vector.reciprocal(rz_t[:, :], pb[2 + d][:, 64:65])
                    vector.drain()
                    vector.tensor_scalar(attnout[:, d * E + h * 64:d * E + (h + 1) * 64],
                                         pb[2 + d][:, 0:64], rz_t[:, :], None, ALU.mult)
                    vector.drain()
                vector.tensor_copy(rz_t[:, :], czero[:, :]).then_inc(sV, 1)
                cV.inc()
            Vv(avev)
            for d in range(4):
                bank_v[2 + d] = cV.v
        return cV.v

    def emit_tail(wa, attend_done_v):
        slot = wa % 2
        for tt in range(4):
            def tp(tensor, tt=tt, attend_done_v=attend_done_v):
                if tt == 0:
                    tensor.wait_ge(sV, attend_done_v)
                if aoT_guard[0]:
                    tensor.wait_ge(sV, aoT_guard[0])
                bank_wait(tensor, 7)
                for fc in range(8):
                    mmi = tensor.transpose(pbf[:, fc * 128:(fc + 1) * 128],
                                     attnout[:, tt * E + fc * 128:tt * E + (fc + 1) * 128],
                                     id_bf[:, :])
                mmi.then_inc(sT, 1)
                cT.inc()
            T(tp)
            tpv = cT.v

            def tev(vector, tpv=tpv):
                vector.wait_ge(sT, tpv)
                vector.tensor_copy(aoT[:, :], pbf[:, :]).then_inc(sV, 1)
                cV.inc()
            Vv(tev)
            bank_v[7] = cV.v
            aov = cV.v

            for half in range(2):
                bank = (tt * 2 + half) % 2

                def omm(tensor, tt=tt, half=half, wa=wa, aov=aov, bank=bank):
                    tensor.wait_ge(sV, aov)
                    if tt == 0 and half == 0:
                        tensor.wait_ge(qW[WO], wload_marks[(min(wa + 1, NW - 1), WO)])
                        tensor.wait_ge(sS, cS.v)
                    bank_wait(tensor, bank)
                    for fc in range(8):
                        mmi = tensor.matmul(pb[bank][:, :],
                                      aoT[:, fc * 128:(fc + 1) * 128],
                                      wslot_ap(1)[:, fc * E + half * 512: fc * E + (half + 1) * 512],
                                      start=(fc == 0), stop=(fc == 7))
                    mmi.then_inc(sT, 1)
                    cT.inc()
                T(omm)
                omv = cT.v

                def oev(vector, tt=tt, half=half, wa=wa, omv=omv, slot=slot, bank=bank):
                    vector.wait_ge(sT, omv)
                    xt = xw[:, (slot * 4 + tt) * E + half * 512:(slot * 4 + tt) * E + (half + 1) * 512]
                    vector.tensor_tensor(pb[bank][:, :], pb[bank][:, :],
                                         boTb[:, half * 512:(half + 1) * 512], ALU.add)
                    vector.drain()
                    vector.tensor_tensor(pb[bank][:, :], pb[bank][:, :],
                                         gaTb[:, half * 512:(half + 1) * 512], ALU.mult)
                    vector.drain()
                    vector.tensor_tensor(xt, xt, pb[bank][:, :], ALU.add)
                    vector.drain()
                    vector.tensor_copy(rz_t[:, :], czero[:, :]).then_inc(sV, 1)
                    cV.inc()
                Vv(oev)
                bank_v[bank] = cV.v
            x1_done = cV.v

            def x1st(scalar, wa=wa, tt=tt, x1_done=x1_done, slot=slot):
                scalar.wait_ge(sV, x1_done)
                scalar.dma_start(out=out_h[wa * WS + tt * 128: wa * WS + (tt + 1) * 128, :],
                                 in_=xw[:, (slot * 4 + tt) * E:(slot * 4 + tt + 1) * E]
                                 ).then_inc(qO, 16)
                cQO.inc(16)
            Sc(x1st)
            x1_written[wa * 4 + tt] = cQO.v

            xt_full = xw[:, (slot * 4 + tt) * E:(slot * 4 + tt + 1) * E]

            def stats2(vector, xt=xt_full, x1_done=x1_done):
                vector.wait_ge(sV, x1_done)
                vector.tensor_reduce(mean_t[:, :], xt, axis=AX.X, op=ALU.add, negate=True)
                vector.tensor_tensor(sq_t[:, :], xt, xt, ALU.mult)
                vector.drain()
                vector.tensor_reduce(var_t[:, :], sq_t[:, :], axis=AX.X, op=ALU.add)
                vector.drain()
                vector.tensor_scalar(mean_t[:, :], mean_t[:, :], crecE[:, :], None, ALU.mult)
                vector.tensor_scalar(var_t[:, :], var_t[:, :], crecE[:, :], None, ALU.mult)
                vector.drain()
                vector.tensor_tensor(sq_t[:, 0:1], mean_t[:, :], mean_t[:, :], ALU.mult)
                vector.drain()
                vector.tensor_tensor(var_t[:, :], var_t[:, :], sq_t[:, 0:1], ALU.subtract).then_inc(sV, 1)
                cV.inc()
            Vv(stats2)
            stv = cV.v

            def rstd2(scalar, stv=stv):
                scalar.wait_ge(sV, stv)
                scalar.activation(out=rstd_t[:, :], in_=var_t[:, :], func=AF.Sqrt,
                                  bias=ceps[:, :]).then_inc(sS, 1)
                cS.inc()
            Sc(rstd2)
            rsv = cS.v

            def norm2(vector, xt=xt_full, tt=tt, rsv=rsv, wa=wa):
                vector.wait_ge(sS, rsv)
                if debug and wa == 0 and tt == 0:
                    vector.wait_ge(qP, 80)
                g = h1t_guard.get(tt % 2)
                if g:
                    vector.wait_ge(sT, g)
                g2 = h2m_norm_guard.get(tt % 2)
                if g2:
                    vector.wait_ge(sG, g2)
                vector.reciprocal(rstd_t[:, :], rstd_t[:, :])
                vector.drain()
                vector.tensor_scalar(h1t[:, (tt % 2) * E:(tt % 2 + 1) * E], xt,
                                     mean_t[:, :], rstd_t[:, :],
                                     ALU.add, ALU.mult).then_inc(sV, 1)
                cV.inc()
            Vv(norm2)
            nv = cV.v

            def h2m_g(gpsimd, tt=tt, nv=nv):
                gpsimd.wait_ge(sV, nv)
                g = h2mt_dma_guard.get(tt % 2)
                if g:
                    gpsimd.wait_ge(qH, g)
                n = h1t[:, (tt % 2) * E:(tt % 2 + 1) * E]
                dst = h2mt[:, (tt % 2) * E:(tt % 2 + 1) * E]
                gpsimd.tensor_tensor(dst, n, cbTb[:, :], ALU.mult)
                gpsimd.drain()
                gpsimd.tensor_tensor(dst, dst, sbTb[:, :], ALU.add)
                gpsimd.drain()
                gpsimd.tensor_tensor(dst, dst, n, ALU.add).then_inc(sG, 1)
                cG.inc()
            Gp(h2m_g)
            h2gv = cG.v
            h2m_norm_guard[tt % 2] = h2gv
            h1tg_g[tt % 2] = h2gv

            def h2mst(scalar, wa=wa, tt=tt, h2gv=h2gv, prevq=cQH.v):
                scalar.wait_ge(sG, h2gv)
                if prevq:
                    scalar.wait_ge(qH, prevq)
                scalar.dma_start(out=h2m_hbm[wa * WS + tt * 128: wa * WS + (tt + 1) * 128, :],
                                 in_=h2mt[:, (tt % 2) * E:(tt % 2 + 1) * E]).then_inc(qH, 16)
                cQH.inc(16)
            Sc(h2mst)
            h2m_written[wa * 4 + tt] = cQH.v
            h2mt_dma_guard[tt % 2] = cQH.v

            def rtp(tensor, tt=tt, h2gv=h2gv):
                tensor.wait_ge(sG, h2gv)
                if aoT_guard[0]:
                    tensor.wait_ge(sV, aoT_guard[0])
                bank_wait(tensor, 7)
                for fc in range(8):
                    mmi = tensor.transpose(pbf[:, fc * 128:(fc + 1) * 128],
                                     h2mt[:, (tt % 2) * E + fc * 128:(tt % 2) * E + (fc + 1) * 128],
                                     id_bf[:, :])
                mmi.then_inc(sT, 1)
                cT.inc()
            T(rtp)
            rtv = cT.v

            def rtev(vector, rtv=rtv):
                vector.wait_ge(sT, rtv)
                vector.tensor_copy(aoT[:, :], pbf[:, :]).then_inc(sV, 1)
                cV.inc()
            Vv(rtev)
            bank_v[7] = cV.v
            rtev_v = cV.v
            aoT_guard[0] = rtev_v

            def logit(tensor, rtev_v=rtev_v):
                tensor.wait_ge(sV, rtev_v)
                bank_wait(tensor, 4)
                tensor.matmul(pb[4][:, 0:8], ones_bf[:, :], bs_bf[:, :],
                              start=True, stop=False)
                for fc in range(8):
                    mmi = tensor.matmul(pb[4][:, 0:8], aoT[:, fc * 128:(fc + 1) * 128],
                                  Wsb[:, fc * 8:(fc + 1) * 8],
                                  start=False, stop=(fc == 7))
                mmi.then_inc(sT, 1)
                cT.inc()
            T(logit)
            lgv = cT.v

            def sfm1(scalar, lgv=lgv):
                scalar.wait_ge(sT, lgv)
                if probe_last_ptv[0]:
                    scalar.wait_ge(sT, probe_last_ptv[0])
                scalar.activation(out=probe_tok[:, :], in_=pb[4][:, 0:8],
                                  func=AF.Exp).then_inc(sS, 1)
                cS.inc()
            Sc(sfm1)
            sf1 = cS.v
            bank_s[4] = sf1

            def sfm2(vector, sf1=sf1):
                vector.wait_ge(sS, sf1)
                vector.tensor_reduce(rz_t[:, :], probe_tok[:, :], axis=AX.X, op=ALU.add)
                vector.drain()
                vector.reciprocal(rz_t[:, :], rz_t[:, :])
                vector.drain()
                vector.tensor_scalar(probe_tok[:, :], probe_tok[:, :], rz_t[:, :],
                                     None, ALU.mult)
                vector.drain()
                vector.tensor_copy(e1_t[:, :], czero[:, :]).then_inc(sV, 1)
                cV.inc()
            Vv(sfm2)
            sf2 = cV.v

            def ptp(tensor, sf2=sf2):
                tensor.wait_ge(sV, sf2)
                bank_wait(tensor, 5)
                tensor.transpose(pb[5][0:8, 0:128], probe_tok[:, :], id_f32[:, :]).then_inc(sT, 1)
                cT.inc()
            T(ptp)
            ptv = cT.v
            probe_last_ptv[0] = ptv

            def pev(vector, wa=wa, tt=tt, ptv=ptv, prevq=probs_sent[0]):
                vector.wait_ge(sT, ptv)
                if prevq:
                    vector.wait_ge(qP, prevq)
                vector.tensor_copy(pstage[:, :], pb[5][0:8, 0:128]).then_inc(sV, 1)
                cV.inc()
            Vv(pev)
            bank_v[5] = cV.v
            pevv = cV.v

            def pst(scalar, wa=wa, tt=tt, pevv=pevv):
                scalar.wait_ge(sV, pevv)
                scalar.dma_start(out=cin_hbm[:, wa * WS + tt * 128: wa * WS + (tt + 1) * 128],
                                 in_=pstage[:, :]).then_inc(qP, 16)
                cQP.inc(16)
            Sc(pst)
            probs_sent[0] = cQP.v

    aoT_guard = [0]
    probe_last_ptv = [0]
    probs_sent = [0]
    h2m_norm_guard = {}
    h2mt_dma_guard = {}

    for it in range(NW + 1):
        w, wa = it, it - 1
        if w < NW:
            pass
            emit_h1(w)
            if debug and w == 0:
                def d1(scalar, g=cV.v):
                    scalar.wait_ge(sV, g)
                    scalar.dma_start(out=dbg_h1T[:, :], in_=h1T[:, :]).then_inc(qP, 16)
                    cQP.inc(16)
                Sc(d1)
            emit_proj_K(w)

            def ldq(sync, w=w):
                load_w(sync, w, WQ, 0, cT.v)
            Sy(ldq)
            if debug and w == 0:
                def d2(scalar, g=cV.v):
                    scalar.wait_ge(sV, g)
                    scalar.dma_start(out=dbg_k[:, :], in_=kroll[:, 0:8 * WS]).then_inc(qP, 16)
                    cQP.inc(16)
                Sc(d2)
            emit_proj_V(w)

            def ldo(sync, w=w):
                load_w(sync, w, WO, 1, cT.v)
            Sy(ldo)
        adv = None
        if debug and w == 0:
            def d3(scalar, g=cV.v):
                scalar.wait_ge(sV, g)
                scalar.dma_start(out=dbg_v[:, :], in_=vroll[:, 0:4 * 16 * 65]).then_inc(qP, 16)
                cQP.inc(16)
            Sc(d3)
        if wa >= 0:
            adv = emit_attend(wa)
        if debug and wa == 0:
            def d5(scalar, g=adv):
                scalar.wait_ge(sV, g)
                scalar.dma_start(out=dbg_ao[:, :], in_=attnout[:, :]).then_inc(qP, 16)
                cQP.inc(16)
            Sc(d5)
        if w < NW:
            emit_proj_Q(w)
        if debug and w == 0:
            def d4(scalar, g=cV.v):
                scalar.wait_ge(sV, g)
                scalar.dma_start(out=dbg_q[:, :], in_=qbuf[:, :]).then_inc(qP, 16)
                cQP.inc(16)
            Sc(d4)
        if wa >= 0:
            emit_tail(wa, adv)
        if w + 1 < NW:
            gates = (cV.v, cS.v, cQO.v)

            def xl(scalar, w=w, gates=gates):
                scalar.wait_ge(sV, gates[0])
                scalar.wait_ge(sS, gates[1])
                scalar.wait_ge(qO, gates[2])
                scalar.dma_start(
                    out=xw[:, ((w + 1) % 2) * 4 * E:(((w + 1) % 2) + 1) * 4 * E].rearrange(
                        "p (t f) -> p t f", t=4),
                    in_=x_h[(w + 1) * WS:(w + 2) * WS, :].rearrange(
                        "(t p) f -> p t f", p=128)).then_inc(qX, 16)
                cQX.inc(16)
            Sc(xl)
            xload_marks[w + 1] = cQX.v
        if w < NW and w + 1 < NW:
            def ldk(sync, w=w):
                load_w(sync, w + 1, WK, 0, cT.v)
            Sy(ldk)

            def ldv2(sync, w=w):
                load_w(sync, w + 1, WV, 1, cT.v)
            Sy(ldv2)

    probsT_done = cV.v
    x1_all_w = cQO.v
    h2m_all_w = cQH.v
    send_done = cQP.v

    if sim_single:
        def p2_coll(scalar):
            scalar.wait_ge(qP, send_done)
            for cc in range(8):
                scalar.dma_start(out=cout_hbm[cc * 8:(cc + 1) * 8, :],
                                 in_=cin_hbm[:, :]).then_inc(qP, 16)
                cQP.inc(16)
        Sc(p2_coll)
        cd = cQP.v

        def p2_collw(gpsimd):
            gpsimd.wait_ge(qP, cd)
            gpsimd.memset(nfb_t[0:1, 0:1], 0.0).then_inc(csem, 1)
        Gp(p2_collw)
    else:
        def p2_coll(gpsimd):
            gpsimd.wait_ge(qP, send_done)
            gpsimd.collective_compute(
                "AllGather", ALU.bypass,
                replica_groups=[list(range(8))],
                ins=[cin_hbm[:, :]], outs=[cout_hbm[:, :]],
            ).then_inc(csem, 1)
        Gp(p2_coll)

    def p2_fetch(sync):
        sync.wait_ge(csem, 1)
        sync.dma_start(out=search[:, :],
                       in_=cout_hbm[:, :].rearrange("r (h f) -> (r h) f", h=2)).then_inc(qS, 16)
        cQS.inc(16)
    Sy(p2_fetch)
    search_done = cQS.v

    bis_t = []
    for itn in range(NITER + 1):
        def upd(vector, itn=itn):
            if itn == 0:
                vector.wait_ge(qS, search_done)
                vector.wait_ge(qO, x1_all_w)
                vector.wait_ge(qH, h2m_all_w)
            else:
                vector.wait_ge(sT, bis_t[-1])
                vector.tensor_scalar(ge_t[:, :], pb[3][:, 0:1], ck[:, :], None, ALU.is_ge)
                vector.drain()
                vector.tensor_tensor(e1_t[:, :], tm_t[:, :], lo_t[:, :], ALU.subtract)
                vector.tensor_tensor(e2_t[:, :], tm_t[:, :], hi_t[:, :], ALU.subtract)
                vector.drain()
                vector.tensor_tensor(e1_t[:, :], ge_t[:, :], e1_t[:, :], ALU.mult)
                vector.tensor_tensor(e2_t[:, :], ge_t[:, :], e2_t[:, :], ALU.mult)
                vector.drain()
                vector.tensor_tensor(lo_t[:, :], lo_t[:, :], e1_t[:, :], ALU.add)
                vector.tensor_tensor(hi_t[:, :], tm_t[:, :], e2_t[:, :], ALU.subtract)
                vector.drain()
                vector.tensor_tensor(tm_t[:, :], lo_t[:, :], hi_t[:, :], ALU.add)
                vector.drain()
                vector.tensor_scalar(tm_t[:, :], tm_t[:, :], chalf[:, :], None, ALU.mult)
                vector.drain()
            if itn < NITER:
                vector.tensor_scalar(bisdump[:, :], search[:, :], tm_t[:, :],
                                     None, ALU.is_gt, ALU.add, accum_out=cnt_t[:, :]).then_inc(sV, 1)
            else:
                vector.tensor_copy(e1_t[:, :], lo_t[:, :]).then_inc(sV, 1)
            cV.inc()
        Vv(upd)
        cntv = cV.v
        if itn > 0:
            bank_v[3] = cntv
        if itn == NITER:
            break

        def red(tensor, cntv=cntv):
            tensor.wait_ge(sV, cntv)
            bank_wait(tensor, 2)
            tensor.matmul(pb[2][0:8, 0:1], selm[:, :], cnt_t[:, :], start=True, stop=True)
            pe_bump(tensor)
        T(red)
        rv_ = cT.v

        def red2(vector, rv_=rv_):
            vector.wait_ge(sT, rv_)
            vector.tensor_copy(cntE[:, :], pb[2][0:8, 0:1]).then_inc(sV, 1)
            cV.inc()
        Vv(red2)
        bank_v[2] = cV.v
        r2v = cV.v

        def red3(tensor, r2v=r2v):
            tensor.wait_ge(sV, r2v)
            bank_wait(tensor, 3)
            tensor.matmul(pb[3][:, 0:1], selmT[:, :], cntE[:, :], start=True, stop=True)
            pe_bump(tensor)
        T(red3)
        bis_t.append(cT.v)
    bisect_done = cV.v

    extm_built = [0]

    def p2_ext(gpsimd):
        gpsimd.iota(ids16[:, :], pattern=[[16, SB]], base=1, channel_multiplier=1,
                    allow_small_or_imprecise_dtypes=True)
        gpsimd.drain()
        gpsimd.tensor_copy(nfb_t[:, :], nfb_t[:, :]).then_inc(sG, 1)
        cG.inc()
    Gp(p2_ext)
    ids_done = cG.v

    def p2_thr(tensor):
        tensor.wait_ge(sV, bisect_done)
        bank_wait(tensor, 2)
        tensor.matmul(pb[2][0:8, 0:1], selm[:, :], lo_t[:, :], start=True, stop=True)
        pe_bump(tensor)
    T(p2_thr)
    thrv = cT.v

    def p2_thr2(vector):
        vector.wait_ge(sT, thrv)
        vector.tensor_scalar(thr8[:, :], pb[2][0:8, 0:1], c16th[:, :], None, ALU.mult)
        vector.drain()
        vector.tensor_copy(e1_t[:, :], czero[:, :]).then_inc(sV, 1)
        cV.inc()
    Vv(p2_thr2)
    bank_v[2] = cV.v
    thr_done = cV.v

    def p3_pload(sync):
        sync.wait_ge(sV, thr_done)
        sync.dma_start(out=probs_sb[:, :], in_=cin_hbm[:, :]).then_inc(qF, 16)
        cQF.inc(16)
    Sy(p3_pload)
    pl_done = cQF.v

    def p3_sel(vector):
        vector.wait_ge(qF, pl_done)
        vector.tensor_scalar(selT[:, :], probs_sb[:, :], thr8[:, :], None, ALU.is_gt).then_inc(sV, 1)
        cV.inc()
    Vv(p3_sel)
    selv = cV.v

    def p3_iota(gpsimd):
        gpsimd.wait_ge(sV, selv)
        gpsimd.iota(vTt[:, :], pattern=[[1, SEQ]], base=1, channel_multiplier=0,
                    allow_small_or_imprecise_dtypes=True)
        gpsimd.drain()
        gpsimd.tensor_copy(nfb_t[:, :], nfb_t[:, :]).then_inc(sG, 1)
        cG.inc()
    Gp(p3_iota)
    iotav = cG.v

    def p3_vt(vector):
        vector.wait_ge(sG, iotav)
        vector.scalar_tensor_tensor(vTt[:, :], probs_sb[:, :], chalf8[:, :], vTt[:, :],
                                    ALU.mult, ALU.add)
        vector.drain()
        vector.tensor_tensor(vTt[:, :], vTt[:, :], selT[:, :], ALU.mult)
        vector.drain()
        vector.tensor_scalar(vTt[:, :], vTt[:, :], cone8[:, :], None, ALU.subtract).then_inc(sV, 1)
        cV.inc()
    Vv(p3_vt)
    vtv = cV.v

    def p3_vtst(scalar):
        scalar.wait_ge(sV, vtv)
        scalar.dma_start(out=vT_hbm[:, :], in_=vTt[:, :]).then_inc(qVT, 16)
        cQVT.inc(16)
    Sc(p3_vtst)
    vt_hbm_done = cQVT.v

    NBLK = 4
    F1B = 8
    STN = [512, 128]
    h2m_all_written = h2m_written[(NW - 1) * 4 + 3]
    x1_all_written = x1_written[(NW - 1) * 4 + 3]
    wblk_marks = {}

    def load_wblk(gp, e, blk, wait_t):
        par = blk % 2
        if wait_t:
            gp.wait_ge(sT, wait_t)
        if cGQ2.v and wait_t:
            gp.wait_ge(gq2, cGQ2.v)
            gp.wait_ge(gq3, cGQ3.v)
        gp.dma_start(out=W1blk[:, par * 8 * KB:(par + 1) * 8 * KB].rearrange("p (d f) -> p d f", d=8),
                     in_=W1_h[e, :, blk * KB:(blk + 1) * KB].rearrange("(d p) f -> p d f", p=128)
                     ).then_inc(gq2, 16)
        cGQ2.inc(16)
        gp.dma_start(out=W2blk[:, par * 8 * KB:(par + 1) * 8 * KB].rearrange("p (f d) -> p f d", f=8),
                     in_=W2_h[e, blk * KB:(blk + 1) * KB, :].rearrange("(f p) d -> p f d", p=128)
                     ).then_inc(gq3, 16)
        cGQ3.inc(16)
        wblk_marks[(e, blk)] = (cGQ2.v, cGQ3.v)

    def p3_wblk0(gpsimd):
        gpsimd.wait_ge(sV, vtv)
        load_wblk(gpsimd, 0, 0, None)
        load_wblk(gpsimd, 0, 1, None)
        wblk_marks[(0, 0)] = wblk_marks[(0, 1)]
    Gp(p3_wblk0)

    gather_marks = {}
    compact_v = {}
    hexp_ready = {}
    rv_ready = {}
    b2_used = [0]
    b2_vused = [0]
    mm2_last_by_par = {}

    def emit_scat(e, scale_done):
        def scat(gpsimd, e=e, scale_done=scale_done):
            gpsimd.wait_ge(sV, scale_done)
            gpsimd.wait_ge(qO, x1_all_written)
            if e > 0:
                gpsimd.wait_ge(ssems[e - 1], 16)
            gpsimd.dma_scatter_add(
                out_h[:, :], outacc[:, :].rearrange("p (a d) -> p a d", a=5),
                idx_all[:, e * CB:(e + 1) * CB],
                num_idxs=C, num_idxs_reg=C, elem_size=E,
                elem_step=out_h.ap[0][0],
            ).then_inc(ssems[e], 16)
        Gp(scat)

    pending_scat = [None]
    for e in range(NEXP):
        def cfetch(sync, e=e):
            sync.wait_ge(qVT, vt_hbm_done)
            if e > 0:
                sync.wait_ge(sG, compact_v[("sg", e - 1)])
            sync.dma_start(out=v16[:, :],
                           in_=vT_hbm[e:e + 1, :].rearrange("r (b a) -> (r a) b", a=16),
                           allow_slow_non_contiguous=True).then_inc(qF, 16)
            cQF.inc(16)
        Sy(cfetch)
        cfv = cQF.v

        def sg(gpsimd, e=e, cfv=cfv):
            gpsimd.wait_ge(qF, cfv)
            gpsimd.wait_ge(sG, ids_done)
            if e > 0:
                gpsimd.wait_ge(sV, compact_v[("rv", e - 1)])
            gpsimd.sparse_gather(cmp16[:, :], v16[:, :], num_found=nf_t[:, :])
            gpsimd.drain()
            gpsimd.tensor_copy(nff_t[:, :], nf_t[:, :])
            gpsimd.drain()
            gpsimd.partition_broadcast(nfb_t[:, :], nff_t[:, :], channels=16).then_inc(sG, 1)
            cG.inc()
        Gp(sg)
        compact_v[("sg", e)] = cG.v
        sgv = cG.v

        def csel(vector, e=e, sgv=sgv):
            vector.wait_ge(sG, sgv)
            if e > 0:
                vector.wait_ge(qC, compact_v[("idx", e - 1)])
            vector.tensor_scalar(m16[:, :], ids16[:, :], nfb_t[:, :], None, ALU.is_le)
            vector.drain()
            vector.select(clean16[:, :], m16[:, :], cmp16[:, :], zz16[:, :], add_drain=True)
            vector.drain()
            vector.tensor_copy(idxlo[:, :], clean16[:, 0:CB])
            vector.drain()
            vector.tensor_copy(idflo[:, :], idxlo[:, :])
            vector.drain()
            vector.tensor_tensor(rv16[:, :], clean16[:, 0:CB], idflo[:, :], ALU.subtract)
            vector.drain()
            vector.tensor_scalar(rv16[:, :], rv16[:, :], ctwo[0:16, :], None, ALU.mult).then_inc(sV, 1)
            cV.inc()
        Vv(csel)
        compact_v[("rv", e)] = cV.v
        cselv = cV.v

        def cidx(sync, e=e, cselv=cselv):
            sync.wait_ge(sV, cselv)
            sync.dma_start(out=idx_all[0:16, e * CB:(e + 1) * CB], in_=idxlo[:, :]).then_inc(qC, 16)
            sync.dma_start(out=idx_all[16:32, e * CB:(e + 1) * CB], in_=idxlo[:, :]).then_inc(qC, 16)
            sync.dma_start(out=rv_hbm[e, :].rearrange("(b a) -> a b", a=16), in_=rv16[:, :],
                           allow_slow_non_contiguous=True).then_inc(qC, 16)
            cQC.inc(48)
            sync.wait_ge(qC, cQC.v)
            sync.dma_start(out=rvsm_all[:, e * (C // 128):(e + 1) * (C // 128)],
                           in_=rv_hbm[e, :].rearrange("(q p) -> p q", p=128),
                           allow_slow_non_contiguous=True).then_inc(qC, 16)
            cQC.inc(16)
        Sy(cidx)
        cidxv = cQC.v
        compact_v[("idx", e)] = cidxv
        rv_ready[e] = cidxv

        def gat(gpsimd, e=e, cidxv=cidxv):
            gpsimd.wait_ge(qC, cidxv)
            gpsimd.wait_ge(qH, h2m_all_written)
            gpsimd.wait_ge(qO, x1_all_written)
            if e >= 2:
                gpsimd.wait_ge(sT, gather_marks[("used", e - 2)])
            gpsimd.dma_gather(
                xinT[:, (e % 2) * 8 * C:((e % 2) + 1) * 8 * C].rearrange("p (a i) -> p a i", a=8),
                h2m_hbm[:, :], idx_all[:, e * CB:(e + 1) * CB],
                num_idxs=C, num_idxs_reg=C, elem_size=E,
                elem_step=h2m_hbm.ap[0][0], transpose=True,
            ).then_inc(gsems[e], 16)
        Gp(gat)
        if pending_scat[0] is not None:
            emit_scat(*pending_scat[0])
            pending_scat[0] = None

        def b2ld(gpsimd, e=e):
            if e > 0:
                gpsimd.wait_ge(sG, b2_used[0])
            gpsimd.dma_start(out=b2row_bf[:, :], in_=b2r_h[e:e + 1, :]).then_inc(gq0, 16)
            cGQ0.inc(16)
        Gp(b2ld)
        b2lv = cGQ0.v
        b2_used[0] = cG.v + 1

        def b2bc(gpsimd, e=e, b2lv=b2lv):
            gpsimd.wait_ge(gq0, b2lv)
            if e > 0:
                gpsimd.wait_ge(sV, b2_vused[0])
            gpsimd.partition_broadcast(b2Tsb[:, :], b2row_bf[0:1, :], channels=128)
            gpsimd.drain()
            gpsimd.memset(nfb_t[0:1, 0:1], 0.0).then_inc(sG, 1)
            cG.inc()
        Gp(b2bc)
        b2done_g = cG.v
        b2_used[0] = b2done_g

        def b2fin(vector, b2done_g=b2done_g):
            vector.wait_ge(sG, b2done_g)
            vector.tensor_copy(e1_t[:, :], czero[:, :]).then_inc(sV, 1)
            cV.inc()
        Vv(b2fin)
        b2done = cV.v

        for blk in range(NBLK):
            par = blk % 2
            if not (e == 0 and blk <= 1):
                pass
            for f1i in range(F1B):
                f1t = blk * F1B + f1i

                def mm1(tensor, e=e, blk=blk, f1i=f1i, f1t=f1t, par=par):
                    if f1i == 0:
                        tensor.wait_ge(gq2, wblk_marks[(e, blk)][0])
                        tensor.wait_ge(gq3, wblk_marks[(e, blk)][1])
                        tensor.wait_ge(gsems[e], 16)
                    for sg_i, (off, n) in enumerate([(0, 512), (512, 128)]):
                        bank_wait(tensor, sg_i)
                        for d in range(8):
                            mmi = tensor.matmul(
                                pb[sg_i][:, 0:n],
                                W1blk[:, par * 8 * KB + d * KB + f1i * 128:
                                      par * 8 * KB + d * KB + (f1i + 1) * 128],
                                xinT[:, (e % 2) * 8 * C + d * C + off:
                                     (e % 2) * 8 * C + d * C + off + n],
                                start=(d == 0), stop=(d == 7))
                    mmi.then_inc(sT, 1)
                    cT.inc()
                T(mm1)
                mm1v = cT.v
                if f1i == F1B - 1 and blk == NBLK - 1:
                    gather_marks[("used", e)] = mm1v

                def gel(scalar, e=e, f1t=f1t, f1i=f1i, mm1v=mm1v, blk=blk):
                    scalar.wait_ge(sT, mm1v)
                    hb = (e % 2) * 8 * C
                    scalar.activation(out=hexp[:, hb + f1i * C: hb + f1i * C + 512],
                                      in_=pb[0][:, 0:512], func=AF.Gelu_apprx_tanh,
                                      bias=b1c[:, e * 32 + f1t:e * 32 + f1t + 1])
                    scalar.activation(out=hexp[:, hb + f1i * C + 512: hb + f1i * C + 640],
                                      in_=pb[1][:, 0:128], func=AF.Gelu_apprx_tanh,
                                      bias=b1c[:, e * 32 + f1t:e * 32 + f1t + 1]).then_inc(sS, 1)
                    cS.inc()
                Sc(gel)
                bank_s[0] = cS.v
                bank_s[1] = cS.v
                gelv = cS.v
                hexp_ready[(e, blk, f1i)] = gelv

            mm2banks = (2, 3) if blk % 2 == 0 else (4, 5)
            for st in range(5):
                def mm2(tensor, e=e, blk=blk, st=st, par=par, mm2banks=mm2banks):
                    tensor.wait_ge(sS, hexp_ready[(e, blk, F1B - 1)])
                    bank_wait(tensor, mm2banks[0])
                    bank_wait(tensor, mm2banks[1])
                    hb = (e % 2) * 8 * C
                    for f1i in range(F1B):
                        for half in range(2):
                            mmi = tensor.matmul(
                                pb[mm2banks[half]][:, :],
                                hexp[:, hb + f1i * C + st * 128: hb + f1i * C + (st + 1) * 128],
                                W2blk[:, par * 8 * KB + f1i * KB + half * 512:
                                      par * 8 * KB + f1i * KB + (half + 1) * 512],
                                start=(f1i == 0), stop=(f1i == F1B - 1))
                    mmi.then_inc(sT, 1)
                    cT.inc()
                T(mm2)
                mm2v = cT.v
                mm2_last_by_par[par] = mm2v

                def oadd(vector, e=e, blk=blk, st=st, mm2v=mm2v, mm2banks=mm2banks, b2done=b2done):
                    vector.wait_ge(sT, mm2v)
                    if blk == 0:
                        vector.wait_ge(sV, b2done)
                        if e > 0:
                            vector.wait_ge(ssems[e - 1], 16)
                    for half in range(2):
                        dst = outacc[:, st * E + half * 512:st * E + (half + 1) * 512]
                        if blk == 0:
                            vector.tensor_tensor(dst, pb[mm2banks[half]][:, :],
                                                 b2Tsb[:, half * 512:(half + 1) * 512], ALU.add)
                        else:
                            vector.tensor_tensor(dst, dst, pb[mm2banks[half]][:, :], ALU.add)
                    vector.drain()
                    if blk == NBLK - 1:
                        vector.wait_ge(qC, rv_ready[e])
                        vector.tensor_scalar(outacc[:, st * E:(st + 1) * E],
                                             outacc[:, st * E:(st + 1) * E],
                                             rvsm_all[:, e * 5 + st:e * 5 + st + 1],
                                             None, ALU.mult)
                        vector.drain()
                        vector.tensor_tensor(outacc[:, st * E:(st + 1) * E],
                                             outacc[:, st * E:(st + 1) * E],
                                             gbTb[:, :], ALU.mult)
                        vector.drain()
                    vector.tensor_copy(e2_t[:, :], czero[:, :]).then_inc(sV, 1)
                    cV.inc()
                Vv(oadd)
                bank_v[mm2banks[0]] = cV.v
                bank_v[mm2banks[1]] = cV.v
                if blk == 0:
                    b2_vused[0] = cV.v
            def ldnext(gpsimd, e=e, blk=blk, gate=cT.v):
                ee, bb = e, blk + 2
                if bb >= NBLK:
                    ee, bb = e + 1, bb - NBLK
                if ee < NEXP:
                    load_wblk(gpsimd, ee, bb, gate)
            Gp(ldnext)
        scale_done = cV.v
        pending_scat[0] = (e, scale_done)

    emit_scat(*pending_scat[0])
    pending_scat[0] = None

    def fin(gpsimd):
        for e in range(NEXP):
            gpsimd.wait_ge(ssems[e], 16)
        gpsimd.wait_ge(qO, x1_all_written)
    Gp(fin)

    def fin2(scalar):
        scalar.wait_ge(ssems[NEXP - 1], 16)
    Sc(fin2)

    es.close()
    return nc


_NC_CACHE = {}


def _get_nc():
    if "nc" not in _NC_CACHE:
        nc = build_nc()
        nc.compile()
        _NC_CACHE["nc"] = nc
    return _NC_CACHE["nc"]


def kernel(L0_attn_x, L0_attn_wq, L0_attn_bq, L0_attn_wk, L0_attn_bk,
           L0_attn_wv, L0_attn_bv, L0_attn_wo, L0_attn_bo,
           L0_ffn_Ws, L0_ffn_bs, L0_ffn_W1, L0_ffn_b1, L0_ffn_W2, L0_ffn_b2,
           L0_c, L0_wmod):
    bf = ml_dtypes.bfloat16
    f32 = np.float32
    asc = np.ascontiguousarray
    x = np.asarray(L0_attn_x, dtype=f32)
    wq, wk, wv, wo = (np.asarray(a, f32) for a in (L0_attn_wq, L0_attn_wk, L0_attn_wv, L0_attn_wo))
    bq, bk, bvv, bo = (np.asarray(a, f32) for a in (L0_attn_bq, L0_attn_bk, L0_attn_bv, L0_attn_bo))
    Ws = np.asarray(L0_ffn_Ws, f32)
    bs = np.asarray(L0_ffn_bs, f32)
    W1 = np.asarray(L0_ffn_W1, f32)
    b1 = np.asarray(L0_ffn_b1, f32)
    W2 = np.asarray(L0_ffn_W2, f32)
    b2 = np.asarray(L0_ffn_b2, f32)
    cc = np.asarray(L0_c, f32)
    wmod = np.asarray(L0_wmod, f32)

    NEG = np.float32(-1.0e30)
    r = np.arange(128)
    tri_l = np.where(r[:, None] >= r[None, :], np.float32(0), NEG).astype(f32)
    tri_u = np.where(r[:, None] <= r[None, :], np.float32(0), NEG).astype(f32)
    selm = np.zeros((128, 8), f32)
    for p in range(128):
        selm[p, (p % 16) // 2] = 1.0
    selmT = asc(selm.T)
    common = {
        "wqT": asc(wq.T.reshape(8, 128, E)),
        "wkT": asc(wk.T.reshape(8, 128, E)),
        "wvT": asc(wv.T.reshape(8, 128, E)),
        "woT": asc(wo.T.reshape(8, 128, E)),
        "bq": asc(bq.reshape(8, 128).T),
        "bk": asc(bk.reshape(8, 128).T),
        "bv_row": bvv.reshape(1, E),
        "bo_row": bo.reshape(1, E),
        "wmod": wmod,
        "Ws": Ws,
        "bs_row": bs.reshape(1, NEXP),
        "W1": W1,
        "b1c": asc(b1.reshape(NEXP, 32, 128).transpose(2, 0, 1).reshape(128, NEXP * 32)),
        "W2": W2,
        "b2r": b2,
        "id_bf": np.eye(128, dtype=bf),
        "id_f32": np.eye(128, dtype=f32),
        "tri_l": tri_l,
        "tri_u": tri_u,
        "selm": selm,
        "selmT": selmT,
        "ones_bf": np.ones((1, 128), dtype=bf),
    }
    in_maps = []
    for core in range(8):
        m = dict(common)
        m["x"] = asc(x[core])
        m["c_col"] = asc(cc[core].reshape(8, 128).T)
        in_maps.append(m)

    nc = _get_nc()
    res = run_bass_kernel_spmd(nc, in_maps, core_ids=list(range(8)))
    out = np.stack([np.asarray(res.results[c]["out"], dtype=f32) for c in range(8)], axis=0)
    return out
